# version 02 (final)
# baseline (speedup 1.0000x reference)
import os

import numpy as np

import concourse.bass as bass
import concourse.mybir as mybir
from concourse.bass_isa import ReduceOp
from concourse.bass_utils import run_bass_kernel_spmd

F32 = mybir.dt.float32
AF = mybir.ActivationFunctionType
ALU = mybir.AluOpType
AX = mybir.AxisListType

D = 1024
KT = D // 128


class Ctx:

    def __init__(self, nc):
        self.nc = nc
        self.sems = {}
        self.cnt = {}

    def sem(self, name):
        if name not in self.sems:
            h = self.nc.semaphore(name).__enter__()
            self.sems[name] = h
            self.cnt[name] = 0
        return self.sems[name]

    def dma(self, q, out, in_, sem):
        h = self.sem(sem)
        q.dma_start(out=out, in_=in_).then_inc(h, 16)
        self.cnt[sem] += 16
        return self.cnt[sem]

    def inc(self, instr, sem, n=1):
        h = self.sem(sem)
        instr.then_inc(h, n)
        self.cnt[sem] += n
        return self.cnt[sem]

    def val(self, sem):
        return self.cnt[sem]

    def wait(self, eng, sem, val=None):
        h = self.sem(sem)
        eng.wait_ge(h, self.cnt[sem] if val is None else val)

    def vop(self, instr):
        v = self.inc(instr, "vec")
        self.nc.vector.wait_ge(self.sems["vec"], v)
        return v

    def barrier_queue(self, q, dma=False):
        sems = ("vec", "act", "pe", "pool") + (("sq", "gq") if dma else ())
        for s in sems:
            if s in self.cnt and self.cnt[s] > 0:
                self.wait(q, s)




def emit_ln_stats(c, ones128, x_tiles, xsq, psA, psB, MU, R, CH, eps=1e-5):
    nc = c.nc
    n_feat = 128 * len(x_tiles)
    inv_n = 1.0 / n_feat
    c.wait(nc.scalar, "vec")
    sq_last = None
    for k, xt in enumerate(x_tiles):
        sq_last = nc.scalar.activation(xsq[k], xt, AF.Square)
    sq_done = c.inc(sq_last, "act")
    c.wait(nc.tensor, "vec")
    c.wait(nc.tensor, "act", sq_done)
    for k, xt in enumerate(x_tiles):
        nc.tensor.matmul(psA[:, :CH], ones128, xt, start=(k == 0), stop=(k == len(x_tiles) - 1))
    mm_last = None
    for k, xt in enumerate(x_tiles):
        mm_last = nc.tensor.matmul(psB[:, :CH], ones128, xsq[k], start=(k == 0), stop=(k == len(x_tiles) - 1))
    mm_done = c.inc(mm_last, "pe")
    c.wait(nc.vector, "pe", mm_done)
    nc.vector.tensor_scalar(MU, psA[:, :CH], inv_n, None, ALU.mult)
    nc.vector.tensor_scalar(R, psB[:, :CH], inv_n, eps, ALU.mult, ALU.add)
    nc.vector.tensor_tensor(xsq[0], MU, MU, ALU.mult)
    var_done = c.inc(nc.vector.tensor_tensor(R, R, xsq[0], ALU.subtract), "vec")
    c.wait(nc.scalar, "vec", var_done)
    sd = c.inc(nc.scalar.activation(R, R, AF.Sqrt), "act")
    c.wait(nc.vector, "act", sd)
    return c.inc(nc.vector.reciprocal(R, R), "vec")




def build_program(T=4096, ncores=8):
    nc = bass.Bass("TRN2", target_bir_lowering=False, detect_race_conditions=False)
    c = Ctx(nc)

    def din(name, shape):
        return nc.dram_tensor(name, shape, F32, kind="ExternalInput").ap()

    dbg = os.environ.get("KK_DBG", "").split(",")

    def dint(name, shape):
        kind = "ExternalOutput" if name in dbg else "Internal"
        return nc.dram_tensor(name, shape, F32, kind=kind).ap()

    xin = din("xin", [D, T])
    yout = nc.dram_tensor("yout", [D, T], F32, kind="ExternalOutput").ap()
    identH = din("identH", [128, 128])

    w_in = din("l0_w_in", [D, 4096])
    b_in = din("l0_b_in", [128, 32])
    convw = din("l0_convw", [128, 16, 3])
    convb = din("l0_convb", [128, 16])
    wabc = [din("l0_wa", [2048, 2048]), din("l0_wb", [2048, 2048]), din("l0_wc", [2048, 2048])]
    babc = din("l0_babc", [3, 128, 16])
    wo0 = din("l0_wo", [2048, D])
    bo0 = din("l0_bo", [128, KT])
    gb0 = din("l0_gb", [128, KT, 2])

    f_sgT = din("l0f_sgT", [D, 2048])
    f_suT = din("l0f_suT", [D, 2048])
    f_sdT = din("l0f_sdT", [2048, D])
    f_gate = din("l0f_gate", [8, D, 2048])
    f_up = din("l0f_up", [8, D, 2048])
    f_down = din("l0f_down", [8, 2048, D])
    f_rwT = din("l0f_rwT", [D, 8])
    f_rb = din("l0f_rb", [1, 8])

    a1_wq = din("l1_wq", [D, D])
    a1_wk = din("l1_wk", [D, D])
    a1_wv = din("l1_wv", [D, D])
    a1_wo = din("l1_wo", [D, D])
    a1_bq = din("l1_bq", [128, KT])
    a1_bk = din("l1_bk", [128, KT])
    a1_bv = din("l1_bv", [1, D])
    a1_bo = din("l1_bo", [128, KT])
    a1_rw2 = din("l1_rw2", [128, 16, 64])
    f1_w = din("l1f_w", [D, D])
    f1_a1 = din("l1f_a1", [D, D])
    f1_a2 = din("l1f_a2", [D, D])
    f1_b = din("l1f_b", [128, KT])
    a2_w = din("l2_w", [D, 3072])
    a2_dmask = din("l2_dmask", [128, 128])
    f2_wg = din("l2f_wg", [3072, D])
    f2_wu = din("l2f_wu", [3072, D])
    f2_wd = din("l2f_wd", [D, 3072])
    a3_wT = din("l3_wT", [D, 3072])
    a3_wout = din("l3_wout", [D, D])
    a3_tril = din("l3_tril", [128, 128])
    a3_triu = din("l3_triu", [128, 128])
    f3_wqkv = din("l3f_wqkv", [2, D, 1536])
    f3_wo = din("l3f_wo", [2, 512, D])
    f3_rotq = din("l3f_rotq", [2, 128, 2, T // 128, 64])
    f3_rotk = din("l3f_rotk", [2, 128, 2, T // 128, 64])
    f3_lgb = din("l3f_lgb", [2, 128, KT, 2])
    f3_w1 = din("l3f_w1", [2, 2, D, 2816])
    f3_b1 = din("l3f_b1", [2, 2, 128, 22])
    f3_w2 = din("l3f_w2", [2, 2730, D])
    f3_b2 = din("l3f_b2", [2, 128, KT])
    f3_dmc = din("l3f_dmc", [128, 128])
    f3_dmf = din("l3f_dmf", [128, 128])

    XH = dint("XH", [D, T])
    S_u = dint("S_u", [2048, T])
    S_a = dint("S_a", [2048, T])
    S_b = dint("S_b", [2048, T])
    S_c = dint("S_c", [2048, T])
    S_xn = dint("S_xn", [D, T])
    S_p = dint("S_p", [2048, T])
    S_acc = dint("S_acc", [D, T])
    S_wT = dint("S_wT", [8, T])
    S_q = dint("S_q", [D, T])
    S_k = dint("S_k", [D, T])
    S_v = dint("S_v", [T, D])
    S_xq = dint("S_xq", [D, T])
    S_up = dint("S_up", [3072, T])
    S_gate = dint("S_gate", [3072, T])
    S_gs = dint("S_gs", [3072, T])
    S_upq = dint("S_upq", [3072, T])
    S_o = dint("S_o", [3072, T])
    S_a3 = dint("S_a3", [2730, T])
    S_p3 = dint("S_p3", [2730, T])
    S_x2 = dint("S_x2", [D, T])
    S_x2b = dint("S_x2b", [D, T])
    CCI = nc.dram_tensor("CCI", [1, 32], F32, kind="Internal")
    CCO = nc.dram_tensor("CCO", [1, 32], F32, kind="Internal")

    ones128_h = nc.sbuf_tensor("ones128", [128, 128], F32).__enter__()
    onesrow_h = nc.sbuf_tensor("onesrow", [1, 512], F32).__enter__()
    ident_h = nc.sbuf_tensor("ident", [128, 128], F32).__enter__()
    ones128, onesrow, ident = ones128_h.ap(), onesrow_h.ap(), ident_h.ap()
    pool = [nc.psum_tensor("pp%d" % i, [128, 512], F32).__enter__().ap() for i in range(8)]

    nc.vector.memset(ones128[:, :], 1.0)
    c.inc(nc.vector.memset(onesrow[:, :], 1.0), "vec")

    c.dma(nc.sync, ident[:, :], identH[:, :], "sq")

    lim = int(os.environ.get("KK_PHASES", "99"))
    blocks = os.environ.get("KK_BLOCKS", "all")
    if lim >= 1:
        if blocks in ("all",) or "l0a" in blocks:
            emit_l0_attn(nc, c, T, pool, ones128, xin, XH,
                         w_in, b_in, convw, convb, wabc, babc, wo0, bo0, gb0,
                         S_u, S_a, S_b, S_c, lim)
        else:
            emit_copy(nc, c, T, xin, XH)
        if blocks in ("all",) or "l0f" in blocks:
            emit_l0_ffn(nc, c, T, pool, ones128, ident, XH,
                        f_sgT, f_suT, f_sdT, f_gate, f_up, f_down, f_rwT, f_rb,
                        S_xn, S_p, S_acc, S_wT)
        if blocks in ("all",) or "l1a" in blocks:
            emit_l1_attn(nc, c, T, pool, ones128, onesrow, XH,
                         a1_wq, a1_wk, a1_wv, a1_wo, a1_bq, a1_bk, a1_bv, a1_bo,
                         a1_rw2, S_xn)
        if blocks in ("all",) or "l1f" in blocks:
            emit_l1_ffn(nc, c, T, pool, ones128, XH, f1_w, f1_a1, f1_a2, f1_b)
        if blocks in ("all",) or "l2a" in blocks:
            emit_l2_attn(nc, c, T, pool, ones128, ident, onesrow, XH, a2_w, a2_dmask,
                         S_xn, S_q, S_k, S_v)
        if blocks in ("all",) or "l2f" in blocks:
            emit_l2_ffn(nc, c, T, pool, ones128, ident, onesrow, XH, f2_wg, f2_wu, f2_wd,
                        S_xn, S_xq, S_up, S_gate, S_gs, S_upq, S_o,
                        CCI.ap(), CCO.ap(), ncores)
        if blocks in ("all",) or "l3a" in blocks:
            emit_l3_attn(nc, c, T, pool, ones128, ident, onesrow, XH, a3_wT, a3_wout,
                         a3_tril, a3_triu, S_xn)
        if blocks in ("all",) or "l3f" in blocks:
            nlyr = int(os.environ.get("KK_L3F_LAYERS", "2"))
            parts = os.environ.get("KK_L3F_PARTS", "af")
            emit_ln_phase(nc, c, T, pool, ones128, XH, S_x2, "l3f")
            for lyr in range(nlyr):
                emit_l3_ffn_layer(nc, c, T, pool, ones128, ident, onesrow, (S_x2, S_x2b), lyr,
                                  f3_wqkv, f3_wo, f3_rotq, f3_rotk, f3_lgb,
                                  f3_w1, f3_b1, f3_w2, f3_b2, f3_dmc, f3_dmf,
                                  S_xn, S_a3, S_p3, parts)
            with (
                nc.sbuf_tensor("l3fin_a", [128, 2, 512], F32) as fa_,
                nc.sbuf_tensor("l3fin_b", [128, 512], F32) as fb_,
            ):
                fa, fb = fa_.ap(), fb_.ap()
                fst = [0, 0]
                for ch in range(T // 512):
                    t0 = ch * 512
                    for k in range(KT):
                        bb = (ch * KT + k) % 2
                        c.barrier_queue(nc.sync, dma=True)
                        c.dma(nc.sync, fa[:, bb, :], XH[k * 128:(k + 1) * 128, t0:t0 + 512], "sq")
                        l2 = c.dma(nc.sync, fb[:, :], S_x2[k * 128:(k + 1) * 128, t0:t0 + 512], "sq")
                        c.wait(nc.vector, "sq", l2)
                        c.wait(nc.vector, "gq", fst[bb])
                        od = c.inc(nc.vector.tensor_tensor(fa[:, bb, :], fa[:, bb, :], fb, ALU.add), "vec")
                        c.wait(nc.gpsimd, "vec", od)
                        fst[bb] = c.dma(nc.gpsimd, XH[k * 128:(k + 1) * 128, t0:t0 + 512], fa[:, bb, :], "gq")
    else:
        emit_copy(nc, c, T, xin, XH)

    CH = 512
    with nc.sbuf_tensor("cp", [128, CH], F32) as cpt_:
        cpt = cpt_.ap()
        for ch in range(T // CH):
            for k in range(KT):
                c.barrier_queue(nc.sync)
                ld = c.dma(nc.sync, cpt[:, :], XH[k * 128:(k + 1) * 128, ch * CH:(ch + 1) * CH], "sq")
                c.wait(nc.vector, "sq", ld)
                vd = c.inc(nc.vector.tensor_copy(cpt[:, :], cpt[:, :]), "vec")
                c.wait(nc.gpsimd, "vec", vd)
                c.dma(nc.gpsimd, yout[k * 128:(k + 1) * 128, ch * CH:(ch + 1) * CH], cpt[:, :], "gq")
                c.wait(nc.sync, "gq")

    c.wait(nc.sync, "sq")
    c.wait(nc.sync, "gq")
    return nc


def emit_copy(nc, c, T, src_h, dst_h):
    name = "cpy%d" % c.cnt.get("_cpy", 0)
    c.cnt["_cpy"] = c.cnt.get("_cpy", 0) + 1
    with nc.sbuf_tensor(name, [128, 512], F32) as sc_:
        sc = sc_.ap()
        for ch in range(T // 512):
            for k in range(KT):
                c.barrier_queue(nc.sync, dma=True)
                ld = c.dma(nc.sync, sc[:, :], src_h[k * 128:(k + 1) * 128, ch * 512:(ch + 1) * 512], "sq")
                c.wait(nc.gpsimd, "sq", ld)
                c.dma(nc.gpsimd, dst_h[k * 128:(k + 1) * 128, ch * 512:(ch + 1) * 512], sc[:, :], "gq")
                c.wait(nc.sync, "gq")




def emit_l0_attn(nc, c, T, pool, ones128, xin, XH,
                 w_in, b_in, convw, convb, wabc, babc, wo0, bo0, gb0,
                 S_u, S_a, S_b, S_c, lim=99):
    CH = 256
    NCH = T // CH
    ITK = 16
    psA, psB, psC, psD = pool[0], pool[1], pool[2], pool[3]

    with (
        nc.sbuf_tensor("p1_w", [128, KT, 4096], F32) as p1w_,
        nc.sbuf_tensor("p1_x", [128, KT, CH], F32) as p1x_,
        nc.sbuf_tensor("p1_sq", [128, KT, CH], F32) as p1sq_,
        nc.sbuf_tensor("p1_xn", [128, KT, CH], F32) as p1xn_,
        nc.sbuf_tensor("p1_mu", [128, CH], F32) as p1mu_,
        nc.sbuf_tensor("p1_r", [128, CH], F32) as p1r_,
        nc.sbuf_tensor("p1_ext", [128, ITK, CH + 2], F32) as p1ext_,
        nc.sbuf_tensor("p1_sig", [128, CH], F32) as p1sig_,
        nc.sbuf_tensor("p1_t1", [128, CH], F32) as p1t1_,
        nc.sbuf_tensor("p1_u", [128, 2, CH], F32) as p1u_,
        nc.sbuf_tensor("p1_bin", [128, 32], F32) as p1bin_,
        nc.sbuf_tensor("p1_cw", [128, 16, 3], F32) as p1cw_,
        nc.sbuf_tensor("p1_cb", [128, 16], F32) as p1cb_,
        nc.sbuf_tensor("p1_gb", [128, KT, 2], F32) as p1gb_,
    ):
        p1w, p1x, p1sq, p1xn = p1w_.ap(), p1x_.ap(), p1sq_.ap(), p1xn_.ap()
        p1mu, p1r, p1ext = p1mu_.ap(), p1r_.ap(), p1ext_.ap()
        p1sig, p1t1, p1u = p1sig_.ap(), p1t1_.ap(), p1u_.ap()
        p1bin, p1cw, p1cb, p1gb = p1bin_.ap(), p1cw_.ap(), p1cb_.ap(), p1gb_.ap()

        c.barrier_queue(nc.sync)
        for k in range(KT):
            c.dma(nc.sync, p1w[:, k, :], w_in[k * 128:(k + 1) * 128, :], "sq")
        c.dma(nc.sync, p1bin[:, :], b_in[:, :], "sq")
        c.dma(nc.sync, p1cw[:, :, :], convw[:, :, :], "sq")
        c.dma(nc.sync, p1cb[:, :], convb[:, :], "sq")
        c.dma(nc.sync, p1gb[:, :, :], gb0[:, :, :], "sq")

        nc.vector.memset(p1ext[:, :, 0:2], 0.0)
        c.inc(nc.vector.memset(p1ext[:, :, CH:CH + 2], 0.0), "vec")

        ustore_mark = [0, 0]

        for ch in range(NCH):
            t0 = ch * CH
            c.barrier_queue(nc.sync)
            for k in range(KT):
                c.dma(nc.sync, p1x[:, k, :], xin[k * 128:(k + 1) * 128, t0:t0 + CH], "sq")
            c.wait(nc.vector, "sq")
            c.wait(nc.scalar, "sq")
            c.wait(nc.tensor, "sq")

            x_tiles = [p1x[:, k, :] for k in range(KT)]
            xsq = [p1sq[:, k, :] for k in range(KT)]
            emit_ln_stats(c, ones128, x_tiles, xsq, psA, psB, p1mu, p1r, CH)
            last = None
            for k in range(KT):
                nc.vector.tensor_tensor(p1xn[:, k, :], p1x[:, k, :], p1mu, ALU.subtract)
                last = nc.vector.tensor_tensor(p1xn[:, k, :], p1xn[:, k, :], p1r, ALU.mult)
            c.inc(last, "vec")
            xn_tiles = [p1xn[:, k, :] for k in range(KT)]
            emit_ln_stats(c, ones128, xn_tiles, xsq, psA, psB, p1mu, p1r, CH)
            last = None
            for k in range(KT):
                nc.vector.tensor_tensor(p1xn[:, k, :], p1xn[:, k, :], p1mu, ALU.subtract)
                nc.vector.tensor_tensor(p1xn[:, k, :], p1xn[:, k, :], p1r, ALU.mult)
                last = nc.vector.tensor_scalar(p1xn[:, k, :], p1xn[:, k, :], p1gb[:, k, 0:1], p1gb[:, k, 1:2], ALU.mult, ALU.add)
            xn_done = c.inc(last, "vec")

            c.wait(nc.vector, "act")
            last = None
            for j in range(ITK):
                last = nc.vector.tensor_copy(p1ext[:, j, 0:2], p1ext[:, j, CH:CH + 2])
            hist_done = c.inc(last, "vec")

            c.wait(nc.tensor, "vec", xn_done)
            for j in range(ITK):
                psg = psC if (j % 2 == 0) else psD
                c.wait(nc.tensor, "act")
                last = None
                for k in range(KT):
                    last = nc.tensor.matmul(psg[:, :CH], p1w[:, k, 2048 + j * 128:2048 + (j + 1) * 128],
                                            p1xn[:, k, :], start=(k == 0), stop=(k == KT - 1))
                gmm = c.inc(last, "pe")
                c.wait(nc.scalar, "pe", gmm)
                sig_done = c.inc(nc.scalar.activation(p1sig, psg[:, :CH], AF.Sigmoid, bias=p1bin[:, 16 + j:17 + j]), "act")

                psp = psD if (j % 2 == 0) else psC
                c.wait(nc.tensor, "act")
                last = None
                for k in range(KT):
                    last = nc.tensor.matmul(psp[:, :CH], p1w[:, k, j * 128:(j + 1) * 128],
                                            p1xn[:, k, :], start=(k == 0), stop=(k == KT - 1))
                pmm = c.inc(last, "pe")
                c.wait(nc.scalar, "pe", pmm)
                c.wait(nc.scalar, "vec", hist_done)
                evac_done = c.inc(nc.scalar.activation(p1ext[:, j, 2:2 + CH], psp[:, :CH], AF.Identity, bias=p1bin[:, j:j + 1]), "act")

                ub = p1u[:, ch % 2, :]
                c.wait(nc.vector, "act", evac_done)
                c.wait(nc.vector, "gq", ustore_mark[ch % 2])
                nc.vector.tensor_scalar(p1t1, p1ext[:, j, 0:CH], p1cw[:, j, 0:1], None, ALU.mult)
                nc.vector.scalar_tensor_tensor(p1t1, p1ext[:, j, 1:1 + CH], p1cw[:, j, 1:2], p1t1, ALU.mult, ALU.add)
                cv = c.inc(nc.vector.scalar_tensor_tensor(p1t1, p1ext[:, j, 2:2 + CH], p1cw[:, j, 2:3], p1t1, ALU.mult, ALU.add), "vec")
                c.wait(nc.scalar, "vec", cv)
                sg = c.inc(nc.scalar.activation(p1t1, p1t1, AF.Silu, bias=p1cb[:, j:j + 1]), "act")
                c.wait(nc.vector, "act", sg)
                ud = c.inc(nc.vector.tensor_tensor(ub, p1t1, p1sig, ALU.mult), "vec")
                c.wait(nc.gpsimd, "vec", ud)
                ustore_mark[ch % 2] = c.dma(nc.gpsimd, S_u[j * 128:(j + 1) * 128, t0:t0 + CH], ub, "gq")
                c.wait(nc.scalar, "vec", ud)

    if lim == 1:
        with nc.sbuf_tensor("sc1", [128, 512], F32) as sc_:
            sc = sc_.ap()
            for ch in range(T // 512):
                for k in range(KT):
                    c.barrier_queue(nc.sync)
                    ld = c.dma(nc.sync, sc[:, :], xin[k * 128:(k + 1) * 128, ch * 512:(ch + 1) * 512], "sq")
                    c.wait(nc.gpsimd, "sq", ld)
                    c.dma(nc.gpsimd, XH[k * 128:(k + 1) * 128, ch * 512:(ch + 1) * 512], sc[:, :], "gq")
                    c.wait(nc.sync, "gq")
        return

    actfns = [AF.Sigmoid, AF.Sigmoid, AF.Tanh]
    outs = [S_a, S_b, S_c]
    for phase in range(3):
        with (
            nc.sbuf_tensor("p2_w%d" % phase, [128, ITK, 2048], F32) as p2w_,
            nc.sbuf_tensor("p2_u%d" % phase, [128, ITK, CH], F32) as p2u_,
            nc.sbuf_tensor("p2_o%d" % phase, [128, 2, CH], F32) as p2o_,
            nc.sbuf_tensor("p2_b%d" % phase, [128, 16], F32) as p2b_,
        ):
            p2w, p2u, p2o, p2b = p2w_.ap(), p2u_.ap(), p2o_.ap(), p2b_.ap()
            c.barrier_queue(nc.sync)
            for k in range(ITK):
                c.dma(nc.sync, p2w[:, k, :], wabc[phase][k * 128:(k + 1) * 128, :], "sq")
            c.dma(nc.sync, p2b[:, :], babc[phase, :, :], "sq")
            ostore_mark = [0, 0]
            for ch in range(NCH):
                t0 = ch * CH
                c.barrier_queue(nc.sync)
                for k in range(ITK):
                    c.dma(nc.sync, p2u[:, k, :], S_u[k * 128:(k + 1) * 128, t0:t0 + CH], "sq")
                c.wait(nc.tensor, "sq")
                for j in range(ITK):
                    ps = pool[j % 4]
                    c.wait(nc.tensor, "act")
                    last = None
                    for k in range(ITK):
                        last = nc.tensor.matmul(ps[:, :CH], p2w[:, k, j * 128:(j + 1) * 128],
                                                p2u[:, k, :], start=(k == 0), stop=(k == ITK - 1))
                    mm = c.inc(last, "pe")
                    ob = p2o[:, j % 2, :]
                    c.wait(nc.scalar, "pe", mm)
                    c.wait(nc.scalar, "gq", ostore_mark[j % 2])
                    ev = c.inc(nc.scalar.activation(ob, ps[:, :CH], actfns[phase], bias=p2b[:, j:j + 1]), "act")
                    c.wait(nc.gpsimd, "act", ev)
                    ostore_mark[j % 2] = c.dma(nc.gpsimd, outs[phase][j * 128:(j + 1) * 128, t0:t0 + CH], ob, "gq")

    if lim == 2:
        with nc.sbuf_tensor("sc2", [128, 512], F32) as sc_:
            sc = sc_.ap()
            for ch in range(T // 512):
                for k in range(KT):
                    c.barrier_queue(nc.sync)
                    ld = c.dma(nc.sync, sc[:, :], xin[k * 128:(k + 1) * 128, ch * 512:(ch + 1) * 512], "sq")
                    c.wait(nc.gpsimd, "sq", ld)
                    c.dma(nc.gpsimd, XH[k * 128:(k + 1) * 128, ch * 512:(ch + 1) * 512], sc[:, :], "gq")
                    c.wait(nc.sync, "gq")
        return

    with (
        nc.sbuf_tensor("p5_w", [128, ITK, D], F32) as p5w_,
        nc.sbuf_tensor("p5_x", [128, KT, CH], F32) as p5x_,
        nc.sbuf_tensor("p5_sq", [128, KT, CH], F32) as p5sq_,
        nc.sbuf_tensor("p5_xn", [128, KT, CH], F32) as p5xn_,
        nc.sbuf_tensor("p5_mu", [128, CH], F32) as p5mu_,
        nc.sbuf_tensor("p5_r", [128, CH], F32) as p5r_,
        nc.sbuf_tensor("p5_u", [128, ITK, CH], F32) as p5u_,
        nc.sbuf_tensor("p5_a", [128, ITK, CH], F32) as p5a_,
        nc.sbuf_tensor("p5_b", [128, ITK, CH], F32) as p5b_,
        nc.sbuf_tensor("p5_c", [128, ITK, CH], F32) as p5c_,
        nc.sbuf_tensor("p5_y", [128, CH], F32) as p5y_,
        nc.sbuf_tensor("p5_s", [128, CH], F32) as p5s_,
        nc.sbuf_tensor("p5_state", [128, ITK], F32) as p5state_,
        nc.sbuf_tensor("p5_out", [128, 2, CH], F32) as p5out_,
        nc.sbuf_tensor("p5_bo", [128, KT], F32) as p5bo_,
    ):
        p5w, p5x, p5sq, p5xn = p5w_.ap(), p5x_.ap(), p5sq_.ap(), p5xn_.ap()
        p5mu, p5r = p5mu_.ap(), p5r_.ap()
        p5u, p5a, p5b, p5c = p5u_.ap(), p5a_.ap(), p5b_.ap(), p5c_.ap()
        p5y, p5s, p5state, p5out, p5bo = p5y_.ap(), p5s_.ap(), p5state_.ap(), p5out_.ap(), p5bo_.ap()

        c.barrier_queue(nc.sync)
        for k in range(ITK):
            c.dma(nc.sync, p5w[:, k, :], wo0[k * 128:(k + 1) * 128, :], "sq")
        c.dma(nc.sync, p5bo[:, :], bo0[:, :], "sq")
        c.inc(nc.vector.memset(p5state[:, :], 0.0), "vec")
        ostore_mark = [0, 0]

        for ch in range(NCH):
            t0 = ch * CH
            c.barrier_queue(nc.sync)
            for k in range(KT):
                c.dma(nc.sync, p5x[:, k, :], xin[k * 128:(k + 1) * 128, t0:t0 + CH], "sq")
            for k in range(ITK):
                c.dma(nc.sync, p5u[:, k, :], S_u[k * 128:(k + 1) * 128, t0:t0 + CH], "sq")
                c.dma(nc.sync, p5a[:, k, :], S_a[k * 128:(k + 1) * 128, t0:t0 + CH], "sq")
                c.dma(nc.sync, p5b[:, k, :], S_b[k * 128:(k + 1) * 128, t0:t0 + CH], "sq")
                c.dma(nc.sync, p5c[:, k, :], S_c[k * 128:(k + 1) * 128, t0:t0 + CH], "sq")
            c.wait(nc.vector, "sq")
            c.wait(nc.scalar, "sq")
            c.wait(nc.tensor, "sq")

            x_tiles = [p5x[:, k, :] for k in range(KT)]
            xsq = [p5sq[:, k, :] for k in range(KT)]
            emit_ln_stats(c, ones128, x_tiles, xsq, psA, psB, p5mu, p5r, CH)
            last = None
            for k in range(KT):
                nc.vector.tensor_tensor(p5xn[:, k, :], p5x[:, k, :], p5mu, ALU.subtract)
                last = nc.vector.tensor_tensor(p5xn[:, k, :], p5xn[:, k, :], p5r, ALU.mult)
            c.inc(last, "vec")

            for k in range(ITK):
                vsm = c.inc(nc.vector.tensor_tensor(p5s, p5b[:, k, :], p5u[:, k, :], ALU.mult), "vec")
                c.wait(nc.vector, "vec", vsm)
                nc.vector.tensor_tensor_scan(p5s, p5a[:, k, :], p5s, p5state[:, k:k + 1], ALU.mult, ALU.add)
                c.vop(nc.vector.tensor_copy(p5state[:, k:k + 1], p5s[:, CH - 1:CH]))
                nc.vector.tensor_tensor(p5y, p5c[:, k, :], p5s, ALU.mult)
                yk = c.inc(nc.vector.tensor_tensor(p5y, p5y, p5u[:, k, :], ALU.add), "vec")
                c.wait(nc.tensor, "vec", yk)
                last = None
                for m in range(KT):
                    last = nc.tensor.matmul(pool[m][:, :CH], p5w[:, k, m * 128:(m + 1) * 128], p5y,
                                            start=(k == 0), stop=(k == ITK - 1))
                mmk = c.inc(last, "pe")
                c.wait(nc.vector, "pe", mmk)

            for m in range(KT):
                ob = p5out[:, m % 2, :]
                c.wait(nc.vector, "gq", ostore_mark[m % 2])
                nc.vector.scalar_tensor_tensor(ob, pool[m][:, :CH], p5bo[:, m:m + 1], p5xn[:, m, :], ALU.add, ALU.add)
                od = c.inc(nc.vector.tensor_tensor(ob, ob, p5x[:, m, :], ALU.add), "vec")
                c.wait(nc.gpsimd, "vec", od)
                ostore_mark[m % 2] = c.dma(nc.gpsimd, XH[m * 128:(m + 1) * 128, t0:t0 + CH], ob, "gq")




def emit_l0_ffn(nc, c, T, pool, ones128, ident, XH,
                f_sgT, f_suT, f_sdT, f_gate, f_up, f_down, f_rwT, f_rb,
                S_xn, S_p, S_acc, S_wT):
    CH = 256
    NCH = T // CH
    ITK = 16
    psA, psB = pool[0], pool[1]

    if True:
        with (
            nc.sbuf_tensor("fr_x", [128, KT, CH], F32) as frx_,
            nc.sbuf_tensor("fr_sq", [128, KT, CH], F32) as frsq_,
            nc.sbuf_tensor("fr_xn", [128, KT, CH], F32) as frxn_,
            nc.sbuf_tensor("fr_mu", [128, CH], F32) as frmu_,
            nc.sbuf_tensor("fr_r", [128, CH], F32) as frr_,
            nc.sbuf_tensor("fr_rw", [128, KT, 8], F32) as frrw_,
            nc.sbuf_tensor("fr_rb", [1, 8], F32) as frrb_,
            nc.sbuf_tensor("fr_p", [128, 8], F32) as frp_,
            nc.sbuf_tensor("fr_t", [128, 8], F32) as frt_,
            nc.sbuf_tensor("fr_s", [128, 4], F32) as frs_,
            nc.sbuf_tensor("fr_w", [128, 8], F32) as frw_,
            nc.sbuf_tensor("fr_wt", [8, 128], F32) as frwt_,
        ):
            frx, frsq, frxn = frx_.ap(), frsq_.ap(), frxn_.ap()
            frmu, frr, frrw, frrb = frmu_.ap(), frr_.ap(), frrw_.ap(), frrb_.ap()
            frp, frt, frs, frw = frp_.ap(), frt_.ap(), frs_.ap(), frw_.ap()
            frwt = frwt_.ap()
            c.barrier_queue(nc.sync, dma=True)
            for k in range(KT):
                c.dma(nc.sync, frrw[:, k, :], f_rwT[k * 128:(k + 1) * 128, :], "sq")
            c.dma(nc.sync, frrb[:, :], f_rb[:, :], "sq")
            for ch in range(NCH):
                t0 = ch * CH
                c.barrier_queue(nc.sync, dma=True)
                for k in range(KT):
                    c.dma(nc.sync, frx[:, k, :], XH[k * 128:(k + 1) * 128, t0:t0 + CH], "sq")
                c.wait(nc.vector, "sq")
                c.wait(nc.scalar, "sq")
                c.wait(nc.tensor, "sq")
                x_tiles = [frx[:, k, :] for k in range(KT)]
                xsq = [frsq[:, k, :] for k in range(KT)]
                emit_ln_stats(c, ones128, x_tiles, xsq, psA, psB, frmu, frr, CH)
                last = None
                for k in range(KT):
                    nc.vector.tensor_tensor(frxn[:, k, :], frx[:, k, :], frmu, ALU.subtract)
                    last = nc.vector.tensor_tensor(frxn[:, k, :], frxn[:, k, :], frr, ALU.mult)
                xn_done = c.inc(last, "vec")
                c.wait(nc.gpsimd, "vec", xn_done)
                for k in range(KT):
                    c.dma(nc.gpsimd, S_xn[k * 128:(k + 1) * 128, t0:t0 + CH], frxn[:, k, :], "gq")

                c.wait(nc.tensor, "vec", xn_done)
                for tt in range(CH // 128):
                    ps = pool[2 + (tt % 2)]
                    c.wait(nc.tensor, "act")
                    c.wait(nc.tensor, "pe")
                    last = None
                    for k in range(KT):
                        last = nc.tensor.matmul(ps[:, 0:8], frxn[:, k, tt * 128:(tt + 1) * 128], frrw[:, k, :],
                                                start=(k == 0), stop=False)
                    last = nc.tensor.matmul(ps[:, 0:8], ones128[0:1, :], frrb[:, :], start=False, stop=True)
                    lg = c.inc(last, "pe")
                    c.wait(nc.scalar, "pe", lg)
                    c.wait(nc.scalar, "vec")
                    sg = c.inc(nc.scalar.activation(frp, ps[:, 0:8], AF.Sigmoid), "act")
                    c.wait(nc.vector, "act", sg)
                    c.vop(nc.vector.tensor_reduce(frs[:, 0:1], frp, AX.X, ALU.max))
                    c.vop(nc.vector.tensor_scalar(frt, frp, frs[:, 0:1], None, ALU.is_equal))
                    c.vop(nc.vector.scalar_tensor_tensor(frt, frt, -2.0, frp, ALU.mult, ALU.add))
                    c.vop(nc.vector.tensor_reduce(frs[:, 1:2], frt, AX.X, ALU.max))
                    c.vop(nc.vector.tensor_scalar(frw, frp, frs[:, 1:2], None, ALU.is_ge))
                    c.vop(nc.vector.tensor_tensor(frw, frw, frp, ALU.mult))
                    c.vop(nc.vector.tensor_tensor(frs[:, 2:3], frs[:, 0:1], frs[:, 1:2], ALU.add))
                    c.vop(nc.vector.reciprocal(frs[:, 3:4], frs[:, 2:3]))
                    wd = c.vop(nc.vector.tensor_scalar(frw, frw, frs[:, 3:4], None, ALU.mult))
                    c.wait(nc.tensor, "vec", wd)
                    c.wait(nc.tensor, "gq")
                    tp = c.inc(nc.tensor.transpose(pool[4][0:8, 0:128], frw, ident), "pe")
                    c.wait(nc.vector, "pe", tp)
                    wt = c.inc(nc.vector.tensor_copy(frwt[:, :], pool[4][0:8, 0:128]), "vec")
                    c.wait(nc.gpsimd, "vec", wt)
                    c.dma(nc.gpsimd, S_wT[:, t0 + tt * 128:t0 + (tt + 1) * 128], frwt[:, :], "gq")

        with (
            nc.sbuf_tensor("fs_w", [128, KT, 4096], F32) as fsw_,
            nc.sbuf_tensor("fs_xn", [128, KT, CH], F32) as fsxn_,
            nc.sbuf_tensor("fs_t", [128, CH], F32) as fst_,
            nc.sbuf_tensor("fs_p", [128, 2, CH], F32) as fsp_,
        ):
            fsw, fsxn, fst, fsp = fsw_.ap(), fsxn_.ap(), fst_.ap(), fsp_.ap()
            c.barrier_queue(nc.sync, dma=True)
            for k in range(KT):
                c.dma(nc.sync, fsw[:, k, 0:2048], f_sgT[k * 128:(k + 1) * 128, :], "sq")
                c.dma(nc.sync, fsw[:, k, 2048:4096], f_suT[k * 128:(k + 1) * 128, :], "sq")
            pstore = [0, 0]
            for ch in range(NCH):
                t0 = ch * CH
                c.barrier_queue(nc.sync, dma=True)
                for k in range(KT):
                    c.dma(nc.sync, fsxn[:, k, :], S_xn[k * 128:(k + 1) * 128, t0:t0 + CH], "sq")
                c.wait(nc.tensor, "sq")
                for j in range(ITK):
                    psg = pool[2 + (j % 2) * 2]
                    psu = pool[3 + (j % 2) * 2]
                    c.wait(nc.tensor, "act")
                    c.wait(nc.tensor, "vec")
                    for k in range(KT):
                        nc.tensor.matmul(psg[:, :CH], fsw[:, k, j * 128:(j + 1) * 128], fsxn[:, k, :],
                                         start=(k == 0), stop=(k == KT - 1))
                    last = None
                    for k in range(KT):
                        last = nc.tensor.matmul(psu[:, :CH], fsw[:, k, 2048 + j * 128:2048 + (j + 1) * 128], fsxn[:, k, :],
                                                start=(k == 0), stop=(k == KT - 1))
                    mm = c.inc(last, "pe")
                    c.wait(nc.scalar, "pe", mm)
                    sl = c.inc(nc.scalar.activation(fst, psg[:, :CH], AF.Silu), "act")
                    ob = fsp[:, j % 2, :]
                    c.wait(nc.vector, "act", sl)
                    c.wait(nc.vector, "gq", pstore[j % 2])
                    pd = c.inc(nc.vector.tensor_tensor(ob, fst, psu[:, :CH], ALU.mult), "vec")
                    c.wait(nc.gpsimd, "vec", pd)
                    pstore[j % 2] = c.dma(nc.gpsimd, S_p[j * 128:(j + 1) * 128, t0:t0 + CH], ob, "gq")

        with (
            nc.sbuf_tensor("fd_w", [128, ITK, D], F32) as fdw_,
            nc.sbuf_tensor("fd_p", [128, ITK, CH], F32) as fdp_,
            nc.sbuf_tensor("fd_o", [128, 2, CH], F32) as fdo_,
        ):
            fdw, fdp, fdo = fdw_.ap(), fdp_.ap(), fdo_.ap()
            c.barrier_queue(nc.sync, dma=True)
            for k in range(ITK):
                c.dma(nc.sync, fdw[:, k, :], f_sdT[k * 128:(k + 1) * 128, :], "sq")
            ostore = [0, 0]
            for ch in range(NCH):
                t0 = ch * CH
                c.barrier_queue(nc.sync, dma=True)
                for k in range(ITK):
                    c.dma(nc.sync, fdp[:, k, :], S_p[k * 128:(k + 1) * 128, t0:t0 + CH], "sq")
                c.wait(nc.tensor, "sq")
                for m in range(KT):
                    ps = pool[2 + (m % 2)]
                    c.wait(nc.tensor, "vec")
                    last = None
                    for k in range(ITK):
                        last = nc.tensor.matmul(ps[:, :CH], fdw[:, k, m * 128:(m + 1) * 128], fdp[:, k, :],
                                                start=(k == 0), stop=(k == ITK - 1))
                    mm = c.inc(last, "pe")
                    ob = fdo[:, m % 2, :]
                    c.wait(nc.vector, "pe", mm)
                    c.wait(nc.vector, "gq", ostore[m % 2])
                    od = c.inc(nc.vector.tensor_copy(ob, ps[:, :CH]), "vec")
                    c.wait(nc.gpsimd, "vec", od)
                    ostore[m % 2] = c.dma(nc.gpsimd, S_acc[m * 128:(m + 1) * 128, t0:t0 + CH], ob, "gq")

        for e in range(8):
            for hf in range(2):
                i0 = hf * 1024
                with (
                    nc.sbuf_tensor("fe_g%d_%d" % (e, hf), [128, KT, 1024], F32) as feg_,
                    nc.sbuf_tensor("fe_u%d_%d" % (e, hf), [128, KT, 1024], F32) as feu_,
                    nc.sbuf_tensor("fe_d%d_%d" % (e, hf), [128, 8, D], F32) as fed_,
                    nc.sbuf_tensor("fe_xn%d_%d" % (e, hf), [128, KT, CH], F32) as fexn_,
                    nc.sbuf_tensor("fe_xw%d_%d" % (e, hf), [128, KT, CH], F32) as fexw_,
                    nc.sbuf_tensor("fe_t%d_%d" % (e, hf), [128, CH], F32) as fet_,
                    nc.sbuf_tensor("fe_pw%d_%d" % (e, hf), [128, 8, CH], F32) as fepw_,
                    nc.sbuf_tensor("fe_acc%d_%d" % (e, hf), [128, KT, CH], F32) as feacc_,
                    nc.sbuf_tensor("fe_wr%d_%d" % (e, hf), [1, CH], F32) as fewr_,
                ):
                    feg, feu, fed = feg_.ap(), feu_.ap(), fed_.ap()
                    fexn, fexw, fet, fepw, feacc = fexn_.ap(), fexw_.ap(), fet_.ap(), fepw_.ap(), feacc_.ap()
                    fewr = fewr_.ap()
                    c.barrier_queue(nc.sync, dma=True)
                    for k in range(KT):
                        c.dma(nc.sync, feg[:, k, :], f_gate[e, k * 128:(k + 1) * 128, i0:i0 + 1024], "sq")
                        c.dma(nc.sync, feu[:, k, :], f_up[e, k * 128:(k + 1) * 128, i0:i0 + 1024], "sq")
                    for k in range(8):
                        c.dma(nc.sync, fed[:, k, :], f_down[e, i0 + k * 128:i0 + (k + 1) * 128, :], "sq")
                    for ch in range(NCH):
                        t0 = ch * CH
                        c.barrier_queue(nc.sync, dma=True)
                        for k in range(KT):
                            c.dma(nc.sync, fexn[:, k, :], S_xn[k * 128:(k + 1) * 128, t0:t0 + CH], "sq")
                            c.dma(nc.sync, feacc[:, k, :], S_acc[k * 128:(k + 1) * 128, t0:t0 + CH], "sq")
                        c.dma(nc.sync, fewr[:, :], S_wT[e:e + 1, t0:t0 + CH], "sq")
                        c.wait(nc.tensor, "sq")
                        c.wait(nc.tensor, "vec")
                        wb = c.inc(nc.tensor.matmul(pool[7][:, :CH], ones128[0:1, :], fewr[:, :],
                                                    start=True, stop=True), "pe")
                        c.wait(nc.vector, "pe", wb)
                        last = None
                        for k in range(KT):
                            last = nc.vector.tensor_tensor(fexw[:, k, :], fexn[:, k, :], pool[7][:, :CH], ALU.mult)
                        xw = c.inc(last, "vec")
                        c.wait(nc.tensor, "vec", xw)
                        for j in range(8):
                            psg = pool[2 + (j % 2) * 2]
                            psu = pool[3 + (j % 2) * 2]
                            c.wait(nc.tensor, "act")
                            c.wait(nc.tensor, "vec")
                            for k in range(KT):
                                nc.tensor.matmul(psg[:, :CH], feg[:, k, j * 128:(j + 1) * 128], fexn[:, k, :],
                                                 start=(k == 0), stop=(k == KT - 1))
                            last = None
                            for k in range(KT):
                                last = nc.tensor.matmul(psu[:, :CH], feu[:, k, j * 128:(j + 1) * 128], fexw[:, k, :],
                                                        start=(k == 0), stop=(k == KT - 1))
                            mm = c.inc(last, "pe")
                            c.wait(nc.scalar, "pe", mm)
                            sl = c.inc(nc.scalar.activation(fet, psg[:, :CH], AF.Silu), "act")
                            c.wait(nc.vector, "act", sl)
                            c.inc(nc.vector.tensor_tensor(fepw[:, j, :], fet, psu[:, :CH], ALU.mult), "vec")
                        astore = 0
                        for m in range(KT):
                            ps = pool[2 + (m % 2)]
                            c.wait(nc.tensor, "vec")
                            last = None
                            for j in range(8):
                                last = nc.tensor.matmul(ps[:, :CH], fed[:, j, m * 128:(m + 1) * 128], fepw[:, j, :],
                                                        start=(j == 0), stop=(j == 7))
                            mm = c.inc(last, "pe")
                            c.wait(nc.vector, "pe", mm)
                            c.wait(nc.vector, "gq", astore)
                            od = c.inc(nc.vector.tensor_tensor(feacc[:, m, :], feacc[:, m, :], ps[:, :CH], ALU.add), "vec")
                            c.wait(nc.gpsimd, "vec", od)
                            astore = c.dma(nc.gpsimd, S_acc[m * 128:(m + 1) * 128, t0:t0 + CH], feacc[:, m, :], "gq")

        with (
            nc.sbuf_tensor("ff_x", [128, 2, 512], F32) as ffx_,
            nc.sbuf_tensor("ff_a", [128, 512], F32) as ffa_,
        ):
            ffx, ffa = ffx_.ap(), ffa_.ap()
            xst = [0, 0]
            for ch in range(T // 512):
                t0 = ch * 512
                for k in range(KT):
                    c.barrier_queue(nc.sync, dma=True)
                    c.dma(nc.sync, ffx[:, (ch * KT + k) % 2, :], XH[k * 128:(k + 1) * 128, t0:t0 + 512], "sq")
                    ld = c.dma(nc.sync, ffa[:, :], S_acc[k * 128:(k + 1) * 128, t0:t0 + 512], "sq")
                    c.wait(nc.vector, "sq", ld)
                    b = (ch * KT + k) % 2
                    c.wait(nc.vector, "gq", xst[b])
                    od = c.inc(nc.vector.tensor_tensor(ffx[:, b, :], ffx[:, b, :], ffa, ALU.add), "vec")
                    c.wait(nc.gpsimd, "vec", od)
                    xst[b] = c.dma(nc.gpsimd, XH[k * 128:(k + 1) * 128, t0:t0 + 512], ffx[:, b, :], "gq")




def emit_ln_phase(nc, c, T, pool, ones128, X_src, S_dst, tag):
    CH = 512
    psA, psB = pool[0], pool[1]
    with (
        nc.sbuf_tensor("ln_x" + tag, [128, KT, CH], F32) as lx_,
        nc.sbuf_tensor("ln_sq" + tag, [128, KT, CH], F32) as lsq_,
        nc.sbuf_tensor("ln_xn" + tag, [128, KT, CH], F32) as lxn_,
        nc.sbuf_tensor("ln_mu" + tag, [128, CH], F32) as lmu_,
        nc.sbuf_tensor("ln_r" + tag, [128, CH], F32) as lr_,
    ):
        lx, lsq, lxn, lmu, lr = lx_.ap(), lsq_.ap(), lxn_.ap(), lmu_.ap(), lr_.ap()
        for ch in range(T // CH):
            t0 = ch * CH
            c.barrier_queue(nc.sync, dma=True)
            for k in range(KT):
                c.dma(nc.sync, lx[:, k, :], X_src[k * 128:(k + 1) * 128, t0:t0 + CH], "sq")
            c.wait(nc.vector, "sq")
            c.wait(nc.scalar, "sq")
            c.wait(nc.tensor, "sq")
            emit_ln_stats(c, ones128, [lx[:, k, :] for k in range(KT)],
                          [lsq[:, k, :] for k in range(KT)], psA, psB, lmu, lr, CH)
            for k in range(KT):
                nc.vector.tensor_tensor(lxn[:, k, :], lx[:, k, :], lmu, ALU.subtract)
                xd = c.inc(nc.vector.tensor_tensor(lxn[:, k, :], lxn[:, k, :], lr, ALU.mult), "vec")
                c.wait(nc.gpsimd, "vec", xd)
                c.dma(nc.gpsimd, S_dst[k * 128:(k + 1) * 128, t0:t0 + CH], lxn[:, k, :], "gq")




def emit_l1_attn(nc, c, T, pool, ones128, onesrow, XH,
                 wq, wk, wv, wo, bq, bk, bv, bo, rw2, S_xn):
    CH = 512
    NCH = T // CH
    NH = 16

    emit_ln_phase(nc, c, T, pool, ones128, XH, S_xn, "l1")

    with (
        nc.sbuf_tensor("l1_kv", [64, 1024], F32) as kvsb_,
        nc.sbuf_tensor("l1_ks", [64, 16], F32) as kssb_,
        nc.sbuf_tensor("l1s_rw", [128, NH, 64], F32) as rwsb_,
        nc.sbuf_tensor("l1_zr", [1, 512], F32) as zr_,
    ):
        kvsb, kssb, rwsb, zr = kvsb_.ap(), kssb_.ap(), rwsb_.ap(), zr_.ap()
        c.barrier_queue(nc.sync, dma=True)
        c.dma(nc.sync, rwsb[:, :, :], rw2[:, :, :], "sq")
        c.inc(nc.vector.memset(zr[:, :], 0.0), "vec")

        with (
            nc.sbuf_tensor("l1s_wk", [128, KT, D], F32) as wks_,
            nc.sbuf_tensor("l1s_wv", [128, KT, D], F32) as wvs_,
            nc.sbuf_tensor("l1s_bk", [128, KT], F32) as bks_,
            nc.sbuf_tensor("l1s_bv", [1, D], F32) as bvs_,
            nc.sbuf_tensor("l1_xn", [128, KT, CH], F32) as xns_,
            nc.sbuf_tensor("l1_k", [128, KT, CH], F32) as ks_,
            nc.sbuf_tensor("l1_v", [128, CH // 128, D], F32) as vs_,
            nc.sbuf_tensor("l1_kp", [128, 64], F32) as kps_,
        ):
            wks, wvs, bks, bvs = wks_.ap(), wvs_.ap(), bks_.ap(), bvs_.ap()
            xns, ks, vs, kps = xns_.ap(), ks_.ap(), vs_.ap(), kps_.ap()
            c.barrier_queue(nc.sync, dma=True)
            for k in range(KT):
                c.dma(nc.sync, wks[:, k, :], wk[k * 128:(k + 1) * 128, :], "sq")
                c.dma(nc.sync, wvs[:, k, :], wv[k * 128:(k + 1) * 128, :], "sq")
            c.dma(nc.sync, bks[:, :], bk[:, :], "sq")
            c.dma(nc.sync, bvs[:, :], bv[:, :], "sq")
            c.wait(nc.tensor, "vec")
            nc.tensor.matmul(pool[6][0:64, :], zr[0:1, 0:64], zr[0:1, 0:512], start=True, stop=False)
            nc.tensor.matmul(pool[7][0:64, :], zr[0:1, 0:64], zr[0:1, 0:512], start=True, stop=False)
            c.inc(nc.tensor.matmul(pool[5][0:64, 0:16], zr[0:1, 0:64], zr[0:1, 0:16], start=True, stop=False), "pe")

            for ch in range(NCH):
                t0 = ch * CH
                c.barrier_queue(nc.sync, dma=True)
                for k in range(KT):
                    c.dma(nc.sync, xns[:, k, :], S_xn[k * 128:(k + 1) * 128, t0:t0 + CH], "sq")
                c.wait(nc.tensor, "sq")
                for m in range(KT):
                    ps = pool[m % 2]
                    c.wait(nc.tensor, "act")
                    last = None
                    for k in range(KT):
                        last = nc.tensor.matmul(ps[:, :CH], wks[:, k, m * 128:(m + 1) * 128], xns[:, k, :],
                                                start=(k == 0), stop=(k == KT - 1))
                    mm = c.inc(last, "pe")
                    c.wait(nc.scalar, "pe", mm)
                    c.inc(nc.scalar.activation(ks[:, m, :], ps[:, :CH], AF.Identity, bias=bks[:, m:m + 1]), "act")
                for tt in range(CH // 128):
                    for n in range(2):
                        ps = pool[2 + ((tt * 2 + n) % 2)]
                        c.wait(nc.tensor, "act")
                        for k in range(KT):
                            nc.tensor.matmul(ps[:, :512], xns[:, k, tt * 128:(tt + 1) * 128],
                                             wvs[:, k, n * 512:(n + 1) * 512], start=(k == 0), stop=False)
                        last = nc.tensor.matmul(ps[:, :512], onesrow[0:1, 0:128], bvs[0:1, n * 512:(n + 1) * 512],
                                                start=False, stop=True)
                        mm = c.inc(last, "pe")
                        c.wait(nc.scalar, "pe", mm)
                        c.inc(nc.scalar.copy(vs[:, tt, n * 512:(n + 1) * 512], ps[:, :512]), "act")
                for tt in range(CH // 128):
                    for h in range(NH):
                        base = (h % 2) * 64
                        m = h // 2
                        ps = pool[4]
                        c.wait(nc.tensor, "vec")
                        c.wait(nc.tensor, "act")
                        kmm = c.inc(nc.tensor.matmul(ps[0:128, 0:64],
                                                     ks[base:base + 64, m, tt * 128:(tt + 1) * 128],
                                                     rwsb[base:base + 64, h, :], start=True, stop=True), "pe")
                        c.wait(nc.vector, "pe", kmm)
                        kpd = c.vop(nc.vector.tensor_scalar(kps, ps[0:128, 0:64], 0.0, 1e-6, ALU.max, ALU.add))
                        c.wait(nc.tensor, "vec", kpd)
                        kvps = pool[6] if h < 8 else pool[7]
                        hh = h % 8
                        nc.tensor.matmul(kvps[0:64, hh * 64:(hh + 1) * 64], kps,
                                         vs[:, tt, h * 64:(h + 1) * 64], start=False, stop=False)
                        c.inc(nc.tensor.matmul(pool[5][0:64, h:h + 1], kps, ones128[:, 0:1],
                                               start=False, stop=False), "pe")
            nc.tensor.matmul(pool[6][0:64, :], zr[0:1, 0:64], zr[0:1, 0:512], start=False, stop=True)
            nc.tensor.matmul(pool[7][0:64, :], zr[0:1, 0:64], zr[0:1, 0:512], start=False, stop=True)
            c.inc(nc.tensor.matmul(pool[5][0:64, 0:16], zr[0:1, 0:64], zr[0:1, 0:16], start=False, stop=True), "pe")
            c.wait(nc.vector, "pe")
            nc.vector.tensor_copy(kvsb[:, 0:512], pool[6][0:64, :])
            nc.vector.tensor_copy(kvsb[:, 512:1024], pool[7][0:64, :])
            c.inc(nc.vector.tensor_copy(kssb[:, :], pool[5][0:64, 0:16]), "vec")

        with (
            nc.sbuf_tensor("l1s_wq", [128, KT, D], F32) as wqs_,
            nc.sbuf_tensor("l1s_wo", [128, KT, D], F32) as wos_,
            nc.sbuf_tensor("l1s_bq", [128, KT], F32) as bqs_,
            nc.sbuf_tensor("l1s_bo", [128, KT], F32) as bos_,
            nc.sbuf_tensor("l1_xn2", [128, KT, CH], F32) as xn2_,
            nc.sbuf_tensor("l1_x2", [128, KT, CH], F32) as x2_,
            nc.sbuf_tensor("l1_q", [128, KT, CH], F32) as qs_,
            nc.sbuf_tensor("l1_qp", [64, CH], F32) as qps_,
            nc.sbuf_tensor("l1_rn", [1, CH], F32) as rns_,
            nc.sbuf_tensor("l1_bc", [64, CH], F32) as bcs_,
            nc.sbuf_tensor("l1_ao", [128, KT, CH], F32) as aos_,
            nc.sbuf_tensor("l1_o", [128, 2, CH], F32) as osb_,
        ):
            wqs, wos, bqs, bos = wqs_.ap(), wos_.ap(), bqs_.ap(), bos_.ap()
            xn2, x2, qs, qps, rns, bcs, aos, osb = (xn2_.ap(), x2_.ap(), qs_.ap(), qps_.ap(),
                                                    rns_.ap(), bcs_.ap(), aos_.ap(), osb_.ap())
            c.barrier_queue(nc.sync, dma=True)
            for k in range(KT):
                c.dma(nc.sync, wqs[:, k, :], wq[k * 128:(k + 1) * 128, :], "sq")
                c.dma(nc.sync, wos[:, k, :], wo[k * 128:(k + 1) * 128, :], "sq")
            c.dma(nc.sync, bqs[:, :], bq[:, :], "sq")
            c.dma(nc.sync, bos[:, :], bo[:, :], "sq")
            ostore = [0, 0]
            for ch in range(NCH):
                t0 = ch * CH
                c.barrier_queue(nc.sync, dma=True)
                for k in range(KT):
                    c.dma(nc.sync, xn2[:, k, :], S_xn[k * 128:(k + 1) * 128, t0:t0 + CH], "sq")
                    c.dma(nc.sync, x2[:, k, :], XH[k * 128:(k + 1) * 128, t0:t0 + CH], "sq")
                c.wait(nc.tensor, "sq")
                for m in range(KT):
                    ps = pool[m % 2]
                    c.wait(nc.tensor, "act")
                    last = None
                    for k in range(KT):
                        last = nc.tensor.matmul(ps[:, :CH], wqs[:, k, m * 128:(m + 1) * 128], xn2[:, k, :],
                                                start=(k == 0), stop=(k == KT - 1))
                    mm = c.inc(last, "pe")
                    c.wait(nc.scalar, "pe", mm)
                    c.inc(nc.scalar.activation(qs[:, m, :], ps[:, :CH], AF.Identity, bias=bqs[:, m:m + 1]), "act")
                for h in range(NH):
                    base = (h % 2) * 64
                    m = h // 2
                    c.wait(nc.tensor, "act")
                    c.wait(nc.tensor, "vec")
                    qmm = c.inc(nc.tensor.matmul(pool[2][0:64, :CH], rwsb[base:base + 64, h, :],
                                                 qs[base:base + 64, m, :], start=True, stop=True), "pe")
                    c.wait(nc.vector, "pe", qmm)
                    qpd = c.inc(nc.vector.tensor_scalar(qps, pool[2][0:64, :CH], 0.0, 1e-6, ALU.max, ALU.add), "vec")
                    c.wait(nc.tensor, "vec", qpd)
                    nc.tensor.matmul(pool[3][0:64, :CH], kvsb[:, h * 64:(h + 1) * 64], qps, start=True, stop=True)
                    nmm = c.inc(nc.tensor.matmul(pool[4][0:1, :CH], kssb[:, h:h + 1], qps, start=True, stop=True), "pe")
                    c.wait(nc.vector, "pe", nmm)
                    c.vop(nc.vector.tensor_scalar(rns, pool[4][0:1, :CH], 1e-6, None, ALU.add))
                    rd = c.vop(nc.vector.reciprocal(rns, rns))
                    c.wait(nc.tensor, "vec", rd)
                    bmm = c.inc(nc.tensor.matmul(pool[5][0:64, :CH], onesrow[0:1, 0:64], rns, start=True, stop=True), "pe")
                    c.wait(nc.vector, "pe", bmm)
                    nc.vector.tensor_copy(bcs, pool[5][0:64, :CH])
                    c.inc(nc.vector.tensor_tensor(aos[base:base + 64, m, :], pool[3][0:64, :CH], bcs, ALU.mult), "vec")
                c.wait(nc.tensor, "vec")
                for m in range(KT):
                    ps = pool[6 + (m % 2)]
                    c.wait(nc.tensor, "vec")
                    last = None
                    for k in range(KT):
                        last = nc.tensor.matmul(ps[:, :CH], wos[:, k, m * 128:(m + 1) * 128], aos[:, k, :],
                                                start=(k == 0), stop=(k == KT - 1))
                    mm = c.inc(last, "pe")
                    ob = osb[:, m % 2, :]
                    c.wait(nc.vector, "pe", mm)
                    c.wait(nc.vector, "gq", ostore[m % 2])
                    nc.vector.scalar_tensor_tensor(ob, ps[:, :CH], bos[:, m:m + 1], x2[:, m, :], ALU.add, ALU.add)
                    od = c.inc(nc.vector.tensor_copy(ob, ob), "vec")
                    c.wait(nc.gpsimd, "vec", od)
                    ostore[m % 2] = c.dma(nc.gpsimd, XH[m * 128:(m + 1) * 128, t0:t0 + CH], ob, "gq")




def emit_l1_ffn(nc, c, T, pool, ones128, XH, w_, a1_, a2_, bias_):
    CH = 512
    NCH = T // CH
    psA, psB = pool[0], pool[1]
    with (
        nc.sbuf_tensor("f1_w", [128, KT, 3 * D], F32) as fw_,
        nc.sbuf_tensor("f1_b", [128, KT], F32) as fb_,
        nc.sbuf_tensor("f1_x", [128, KT, CH], F32) as fx_,
        nc.sbuf_tensor("f1_sq", [128, KT, CH], F32) as fsq_,
        nc.sbuf_tensor("f1_h1", [128, KT, CH], F32) as fh1_,
        nc.sbuf_tensor("f1_h2", [128, KT, CH], F32) as fh2_,
        nc.sbuf_tensor("f1_mu", [128, CH], F32) as fmu_,
        nc.sbuf_tensor("f1_r", [128, CH], F32) as fr_,
        nc.sbuf_tensor("f1_o", [128, 2, CH], F32) as fo_,
    ):
        fw, fb, fx, fsq = fw_.ap(), fb_.ap(), fx_.ap(), fsq_.ap()
        fh1, fh2, fmu, fr, fo = fh1_.ap(), fh2_.ap(), fmu_.ap(), fr_.ap(), fo_.ap()
        c.barrier_queue(nc.sync, dma=True)
        for k in range(KT):
            c.dma(nc.sync, fw[:, k, 0:D], w_[k * 128:(k + 1) * 128, :], "sq")
            c.dma(nc.sync, fw[:, k, D:2 * D], a1_[k * 128:(k + 1) * 128, :], "sq")
            c.dma(nc.sync, fw[:, k, 2 * D:3 * D], a2_[k * 128:(k + 1) * 128, :], "sq")
        c.dma(nc.sync, fb[:, :], bias_[:, :], "sq")
        ostore = [0, 0]
        for ch in range(NCH):
            t0 = ch * CH
            c.barrier_queue(nc.sync, dma=True)
            for k in range(KT):
                c.dma(nc.sync, fx[:, k, :], XH[k * 128:(k + 1) * 128, t0:t0 + CH], "sq")
            c.wait(nc.vector, "sq")
            c.wait(nc.scalar, "sq")
            c.wait(nc.tensor, "sq")
            emit_ln_stats(c, ones128, [fx[:, k, :] for k in range(KT)],
                          [fsq[:, k, :] for k in range(KT)], psA, psB, fmu, fr, CH)
            last = None
            for k in range(KT):
                nc.vector.tensor_tensor(fh1[:, k, :], fx[:, k, :], fmu, ALU.subtract)
                last = nc.vector.tensor_tensor(fh1[:, k, :], fh1[:, k, :], fr, ALU.mult)
            xnd = c.inc(last, "vec")
            c.wait(nc.tensor, "vec", xnd)
            for step, (src_t, dst_t, off) in enumerate([(fh1, fh2, 0), (fh2, fh1, D)]):
                for m in range(KT):
                    ps = pool[2 + (m % 2)]
                    c.wait(nc.tensor, "act")
                    last = None
                    for k in range(KT):
                        last = nc.tensor.matmul(ps[:, :CH], fw[:, k, off + m * 128:off + (m + 1) * 128],
                                                src_t[:, k, :], start=(k == 0), stop=(k == KT - 1))
                    mm = c.inc(last, "pe")
                    c.wait(nc.scalar, "pe", mm)
                    c.inc(nc.scalar.copy(dst_t[:, m, :], ps[:, :CH]), "act")
                c.wait(nc.tensor, "act")
            for m in range(KT):
                ps = pool[2 + (m % 2)]
                c.wait(nc.tensor, "act")
                c.wait(nc.tensor, "vec")
                last = None
                for k in range(KT):
                    last = nc.tensor.matmul(ps[:, :CH], fw[:, k, 2 * D + m * 128:2 * D + (m + 1) * 128],
                                            fh1[:, k, :], start=(k == 0), stop=(k == KT - 1))
                mm = c.inc(last, "pe")
                c.wait(nc.scalar, "pe", mm)
                th = c.inc(nc.scalar.activation(fh2[:, m, :], ps[:, :CH], AF.Tanh, bias=fb[:, m:m + 1]), "act")
                ob = fo[:, m % 2, :]
                c.wait(nc.vector, "act", th)
                c.wait(nc.vector, "gq", ostore[m % 2])
                od = c.inc(nc.vector.tensor_tensor(ob, fh2[:, m, :], fx[:, m, :], ALU.add), "vec")
                c.wait(nc.gpsimd, "vec", od)
                ostore[m % 2] = c.dma(nc.gpsimd, XH[m * 128:(m + 1) * 128, t0:t0 + CH], ob, "gq")




def emit_l2_attn(nc, c, T, pool, ones128, ident, onesrow, XH, w2a, dmask, S_xn, S_q, S_k, S_v):
    CH = 512
    NCH = T // CH
    QB = min(1024, T)
    NQB = T // QB

    emit_ln_phase(nc, c, T, pool, ones128, XH, S_xn, "l2a")

    with (
        nc.sbuf_tensor("a2_w", [128, KT, 3072], F32) as aw_,
        nc.sbuf_tensor("a2_xn", [128, KT, CH], F32) as axn_,
        nc.sbuf_tensor("a2_o", [128, 2, CH], F32) as ao_,
        nc.sbuf_tensor("a2_vo", [128, 2, 512], F32) as avo_,
    ):
        aw, axn, ao, avo = aw_.ap(), axn_.ap(), ao_.ap(), avo_.ap()
        c.barrier_queue(nc.sync, dma=True)
        for k in range(KT):
            c.dma(nc.sync, aw[:, k, :], w2a[k * 128:(k + 1) * 128, :], "sq")
        ostore = [0, 0]
        vstore = [0, 0]
        for ch in range(NCH):
            t0 = ch * CH
            c.barrier_queue(nc.sync, dma=True)
            for k in range(KT):
                c.dma(nc.sync, axn[:, k, :], S_xn[k * 128:(k + 1) * 128, t0:t0 + CH], "sq")
            c.wait(nc.tensor, "sq")
            for qk in range(2):
                dst = S_q if qk == 0 else S_k
                for m in range(KT):
                    ps = pool[m % 2]
                    c.wait(nc.tensor, "act")
                    last = None
                    for k in range(KT):
                        last = nc.tensor.matmul(ps[:, :CH], aw[:, k, qk * D + m * 128:qk * D + (m + 1) * 128],
                                                axn[:, k, :], start=(k == 0), stop=(k == KT - 1))
                    mm = c.inc(last, "pe")
                    ob = ao[:, m % 2, :]
                    c.wait(nc.scalar, "pe", mm)
                    c.wait(nc.scalar, "gq", ostore[m % 2])
                    ev = c.inc(nc.scalar.copy(ob, ps[:, :CH]), "act")
                    c.wait(nc.gpsimd, "act", ev)
                    ostore[m % 2] = c.dma(nc.gpsimd, dst[m * 128:(m + 1) * 128, t0:t0 + CH], ob, "gq")
            for tt in range(CH // 128):
                for n in range(2):
                    b = (tt * 2 + n) % 2
                    ps = pool[2 + b]
                    c.wait(nc.tensor, "act")
                    last = None
                    for k in range(KT):
                        last = nc.tensor.matmul(ps[:, :512], axn[:, k, tt * 128:(tt + 1) * 128],
                                                aw[:, k, 2048 + n * 512:2048 + (n + 1) * 512],
                                                start=(k == 0), stop=(k == KT - 1))
                    mm = c.inc(last, "pe")
                    ob = avo[:, b, :]
                    c.wait(nc.scalar, "pe", mm)
                    c.wait(nc.scalar, "gq", vstore[b])
                    ev = c.inc(nc.scalar.copy(ob, ps[:, :512]), "act")
                    c.wait(nc.gpsimd, "act", ev)
                    vstore[b] = c.dma(nc.gpsimd, S_v[t0 + tt * 128:t0 + (tt + 1) * 128, n * 512:(n + 1) * 512], ob, "gq")

    with (
        nc.sbuf_tensor("a2_q", [128, KT, QB], F32) as q_,
        nc.sbuf_tensor("a2_k", [128, KT, 2, 512], F32) as kk_,
        nc.sbuf_tensor("a2_v", [128, 4, 2, D], F32) as vv_,
        nc.sbuf_tensor("a2_acc", [128, QB // 128, D], F32) as acc_,
        nc.sbuf_tensor("a2_M", [128, QB // 128], F32) as M_,
        nc.sbuf_tensor("a2_L", [128, QB // 128], F32) as L_,
        nc.sbuf_tensor("a2_S", [128, 512], F32) as S_,
        nc.sbuf_tensor("a2_P", [128, 512], F32) as P_,
        nc.sbuf_tensor("a2_PT", [128, 4, 128], F32) as PT_,
        nc.sbuf_tensor("a2_t", [128, 8], F32) as t_,
        nc.sbuf_tensor("a2_msk", [128, 128], F32) as msk_,
        nc.sbuf_tensor("a2_xo", [128, 128], F32) as xo_,
    ):
        q, kk, vv, acc = q_.ap(), kk_.ap(), vv_.ap(), acc_.ap()
        M, L, S, P, PT, tb, msk, xo = (M_.ap(), L_.ap(), S_.ap(), P_.ap(), PT_.ap(),
                                       t_.ap(), msk_.ap(), xo_.ap())
        c.barrier_queue(nc.sync, dma=True)
        c.dma(nc.sync, msk[:, :], dmask[:, :], "sq")
        NT = QB // 128
        for qb in range(NQB):
            q0 = qb * QB
            c.barrier_queue(nc.sync, dma=True)
            for k in range(KT):
                c.dma(nc.sync, q[:, k, :], S_q[k * 128:(k + 1) * 128, q0:q0 + QB], "sq")
            kbs = list(range(2 * qb, T // 512))
            for kbi, kb in enumerate(kbs):
                j0 = kb * 512
                buf = kbi % 2
                c.barrier_queue(nc.sync, dma=True)
                for k in range(KT):
                    c.dma(nc.sync, kk[:, k, buf, :], S_k[k * 128:(k + 1) * 128, j0:j0 + 512], "sq")
                for tt4 in range(4):
                    c.dma(nc.sync, vv[:, tt4, buf, :], S_v[j0 + tt4 * 128:j0 + (tt4 + 1) * 128, :], "sq")
                c.wait(nc.tensor, "sq")
                for tt in range(NT):
                    qi = q0 + tt * 128
                    if j0 + 512 <= qi:
                        continue
                    first = (kb == qi // 512)
                    ps = pool[(tt % 2)]
                    c.wait(nc.tensor, "vec")
                    c.wait(nc.tensor, "act")
                    last = None
                    for k in range(KT):
                        last = nc.tensor.matmul(ps[:, :512], q[:, k, tt * 128:(tt + 1) * 128],
                                                kk[:, k, buf, :], start=(k == 0), stop=(k == KT - 1))
                    mm = c.inc(last, "pe")
                    c.wait(nc.vector, "pe", mm)
                    nc.vector.tensor_scalar(S, ps[:, :512], 1.0 / 32.0, None, ALU.mult)
                    if j0 <= qi < j0 + 512:
                        off = qi - j0
                        if off > 0:
                            nc.vector.memset(S[:, 0:off], -9.0e15)
                        nc.vector.tensor_tensor(S[:, off:off + 128], S[:, off:off + 128], msk, ALU.add)
                    c.vop(nc.vector.tensor_reduce(tb[:, 0:1], S, AX.X, ALU.max))
                    if first:
                        c.vop(nc.vector.tensor_copy(tb[:, 1:2], tb[:, 0:1]))
                    else:
                        c.vop(nc.vector.tensor_tensor(tb[:, 1:2], tb[:, 0:1], M[:, tt:tt + 1], ALU.max))
                        c.vop(nc.vector.tensor_tensor(tb[:, 2:3], M[:, tt:tt + 1], tb[:, 1:2], ALU.subtract))
                    c.vop(nc.vector.tensor_scalar(tb[:, 4:5], tb[:, 1:2], -1.0, None, ALU.mult))
                    sd = c.vop(nc.vector.tensor_copy(M[:, tt:tt + 1], tb[:, 1:2]))
                    c.wait(nc.scalar, "vec", sd)
                    if not first:
                        nc.scalar.activation(tb[:, 3:4], tb[:, 2:3], AF.Exp)
                    pe_ = c.inc(nc.scalar.activation(P, S, AF.Exp, bias=tb[:, 4:5], accum_out=tb[:, 5:6]), "act")
                    c.wait(nc.vector, "act", pe_)
                    if first:
                        al = c.vop(nc.vector.tensor_copy(L[:, tt:tt + 1], tb[:, 5:6]))
                    else:
                        c.vop(nc.vector.scalar_tensor_tensor(L[:, tt:tt + 1], L[:, tt:tt + 1], tb[:, 3:4], tb[:, 5:6], ALU.mult, ALU.add))
                        al = c.inc(nc.vector.tensor_scalar(acc[:, tt, :], acc[:, tt, :], tb[:, 3:4], None, ALU.mult), "vec")
                    c.wait(nc.tensor, "vec", al)
                    for js in range(4):
                        tp = c.inc(nc.tensor.transpose(pool[2][:, 0:128], P[:, js * 128:(js + 1) * 128], ident), "pe")
                        c.wait(nc.scalar, "pe", tp)
                        cp = c.inc(nc.scalar.copy(PT[:, js, :], pool[2][:, 0:128]), "act")
                        c.wait(nc.tensor, "act", cp)
                    for n in range(2):
                        psv = pool[3 + n]
                        last = None
                        for js in range(4):
                            last = nc.tensor.matmul(psv[:, :512], PT[:, js, :], vv[:, js, buf, n * 512:(n + 1) * 512],
                                                    start=(js == 0), stop=(js == 3))
                        pv = c.inc(last, "pe")
                        c.wait(nc.vector, "pe", pv)
                        if first:
                            c.inc(nc.vector.tensor_copy(acc[:, tt, n * 512:(n + 1) * 512], psv[:, :512]), "vec")
                        else:
                            c.inc(nc.vector.tensor_tensor(acc[:, tt, n * 512:(n + 1) * 512],
                                                          acc[:, tt, n * 512:(n + 1) * 512], psv[:, :512], ALU.add), "vec")
            for tt in range(NT):
                qi = q0 + tt * 128
                rd = c.inc(nc.vector.reciprocal(tb[:, 6:7], L[:, tt:tt + 1]), "vec")
                c.wait(nc.vector, "vec", rd)
                fd = c.inc(nc.vector.tensor_scalar(acc[:, tt, :], acc[:, tt, :], tb[:, 6:7], None, ALU.mult), "vec")
                c.wait(nc.tensor, "vec", fd)
                for es in range(KT):
                    c.barrier_queue(nc.sync, dma=True)
                    xl = c.dma(nc.sync, xo[:, :], XH[es * 128:(es + 1) * 128, qi:qi + 128], "sq")
                    c.wait(nc.tensor, "gq")
                    tp = c.inc(nc.tensor.transpose(pool[5][:, 0:128], acc[:, tt, es * 128:(es + 1) * 128], ident), "pe")
                    c.wait(nc.vector, "pe", tp)
                    c.wait(nc.vector, "sq", xl)
                    od = c.inc(nc.vector.tensor_tensor(xo[:, :], xo[:, :], pool[5][:, 0:128], ALU.add), "vec")
                    c.wait(nc.gpsimd, "vec", od)
                    c.dma(nc.gpsimd, XH[es * 128:(es + 1) * 128, qi:qi + 128], xo[:, :], "gq")




MAGIC = 12582912.0


def _emit_qdq(nc, c, tile, inv_s, s, tmp, clip=None):
    nc.vector.tensor_scalar(tile, tile, inv_s, None, ALU.mult)
    nc.vector.tensor_scalar(tile, tile, MAGIC, -MAGIC, ALU.add, ALU.add)
    if clip is not None:
        nc.vector.tensor_scalar(tile, tile, clip[0], clip[1], ALU.max, ALU.min)
    return nc.vector.tensor_scalar(tile, tile, s, None, ALU.mult)


def emit_l2_ffn(nc, c, T, pool, ones128, ident, onesrow, XH, wg_h, wu_h, wd_h,
                S_xn, S_xq, S_up, S_gate, S_gs, S_upq, S_o,
                cci, cco, ncores):
    CH = 512
    NCH = T // CH
    I3 = 3072
    IT3 = I3 // 128
    psA, psB = pool[0], pool[1]
    rg = [list(range(ncores))]

    emit_ln_phase(nc, c, T, pool, ones128, XH, S_xn, "l2f")

    with (
        nc.sbuf_tensor("qf_par", [1, 16], F32) as qpar_,
        nc.sbuf_tensor("qf_parb", [128, 8], F32) as qparb_,
        nc.sbuf_tensor("qf_mx", [128, 4], F32) as qmx_,
        nc.sbuf_tensor("qf_cc", [1, 32], F32) as qcc_,
    ):
        qpar, qparb, qmx, qcc = qpar_.ap(), qparb_.ap(), qmx_.ap(), qcc_.ap()

        def global_max(col_src, n_vals, dst_cols):
            c.wait(nc.tensor, "vec")
            c.wait(nc.tensor, "act")
            tp = c.inc(nc.tensor.transpose(pool[2][0:n_vals, 0:128],
                                           qmx[:, col_src:col_src + n_vals], ident), "pe")
            c.wait(nc.vector, "pe", tp)
            rm = c.vop(nc.vector.tensor_reduce(qmx[0:n_vals, col_src:col_src + 1],
                                               pool[2][0:n_vals, 0:128], AX.X, ALU.max))
            c.wait(nc.tensor, "vec", rm)
            tp2 = c.inc(nc.tensor.transpose(pool[2][0:1, 0:n_vals],
                                            qmx[0:n_vals, col_src:col_src + 1],
                                            ident[0:n_vals, 0:n_vals]), "pe")
            c.wait(nc.vector, "pe", tp2)
            c.vop(nc.vector.memset(qcc[:, :], 0.0))
            cp = c.vop(nc.vector.tensor_copy(qcc[0:1, 0:n_vals], pool[2][0:1, 0:n_vals]))
            c.wait(nc.gpsimd, "vec", cp)
            c.dma(nc.gpsimd, cci[:, :], qcc[:, :], "gq")
            c.wait(nc.gpsimd, "gq")
            cc = c.inc(nc.gpsimd.collective_compute("AllReduce", ALU.max, replica_groups=rg,
                                                    ins=[cci.opt()], outs=[cco.opt()]), "pool")
            c.wait(nc.gpsimd, "pool", cc)
            ld = c.dma(nc.gpsimd, qcc[:, :], cco[:, :], "gq")
            c.wait(nc.vector, "gq", ld)
            for i in range(n_vals):
                c.vop(nc.vector.tensor_copy(qpar[0:1, dst_cols[i]:dst_cols[i] + 1], qcc[0:1, i:i + 1]))
            return c.val("vec")

        def make_scale(src_col, s_col, inv_col, qmax):
            c.wait(nc.vector, "vec")
            c.vop(nc.vector.tensor_scalar(qpar[0:1, s_col:s_col + 1], qpar[0:1, src_col:src_col + 1],
                                          1.0 / qmax, 1e-12, ALU.mult, ALU.max))
            sd = c.vop(nc.vector.reciprocal(qpar[0:1, inv_col:inv_col + 1], qpar[0:1, s_col:s_col + 1]))
            c.wait(nc.tensor, "vec", sd)
            nc.tensor.matmul(pool[2][:, 0:1], onesrow[0:1, 0:128], qpar[0:1, s_col:s_col + 1],
                             start=True, stop=True)
            bb = c.inc(nc.tensor.matmul(pool[2][:, 1:2], onesrow[0:1, 0:128], qpar[0:1, inv_col:inv_col + 1],
                                        start=True, stop=True), "pe")
            c.wait(nc.vector, "pe", bb)
            nc.vector.tensor_copy(qparb[:, s_col:s_col + 1], pool[2][:, 0:1])
            c.inc(nc.vector.tensor_copy(qparb[:, inv_col:inv_col + 1], pool[2][:, 1:2]), "vec")
            c.wait(nc.vector, "vec")

        with (
            nc.sbuf_tensor("q1_x", [128, KT, CH], F32) as q1x_,
            nc.sbuf_tensor("q1_t", [128, 1], F32) as q1t_,
        ):
            q1x, q1t = q1x_.ap(), q1t_.ap()
            first = True
            for ch in range(NCH):
                t0 = ch * CH
                c.barrier_queue(nc.sync, dma=True)
                for k in range(KT):
                    c.dma(nc.sync, q1x[:, k, :], S_xn[k * 128:(k + 1) * 128, t0:t0 + CH], "sq")
                c.wait(nc.vector, "sq")
                for k in range(KT):
                    c.vop(nc.vector.tensor_reduce(q1t, q1x[:, k, :], AX.X, ALU.max, apply_absolute_value=True))
                    if first:
                        c.vop(nc.vector.tensor_copy(qmx[:, 0:1], q1t))
                        first = False
                    else:
                        c.vop(nc.vector.tensor_tensor(qmx[:, 0:1], qmx[:, 0:1], q1t, ALU.max))
            c.inc(nc.vector.tensor_copy(qmx[:, 0:1], qmx[:, 0:1]), "vec")
        global_max(0, 1, [0])
        make_scale(0, 1, 2, 32767.0)

        with (
            nc.sbuf_tensor("q2_x", [128, KT, CH], F32) as q2x_,
            nc.sbuf_tensor("q2_t", [128, CH], F32) as q2t_,
        ):
            q2x, q2t = q2x_.ap(), q2t_.ap()
            for ch in range(NCH):
                t0 = ch * CH
                c.barrier_queue(nc.sync, dma=True)
                for k in range(KT):
                    c.dma(nc.sync, q2x[:, k, :], S_xn[k * 128:(k + 1) * 128, t0:t0 + CH], "sq")
                c.wait(nc.vector, "sq")
                for k in range(KT):
                    qd = c.inc(_emit_qdq(nc, c, q2x[:, k, :], qparb[:, 2:3], qparb[:, 1:2], q2t), "vec")
                    c.wait(nc.gpsimd, "vec", qd)
                    c.dma(nc.gpsimd, S_xq[k * 128:(k + 1) * 128, t0:t0 + CH], q2x[:, k, :], "gq")

        for wi, (w_h, S_dst, mxcol) in enumerate([(wu_h, S_up, 1), (wg_h, S_gate, 2)]):
            with (
                nc.sbuf_tensor("q3w_%d" % wi, [128, KT, I3], F32) as q3w_,
                nc.sbuf_tensor("q3raw_%d" % wi, [128, 1024], F32) as q3raw_,
                nc.sbuf_tensor("q3s_%d" % wi, [128, 32], F32) as q3s_,
                nc.sbuf_tensor("q3is_%d" % wi, [128, 32], F32) as q3is_,
                nc.sbuf_tensor("q3t_%d" % wi, [128, 1024], F32) as q3t_,
                nc.sbuf_tensor("q3x_%d" % wi, [128, KT, CH], F32) as q3x_,
                nc.sbuf_tensor("q3o_%d" % wi, [128, 2, CH], F32) as q3o_,
                nc.sbuf_tensor("q3m_%d" % wi, [128, 1], F32) as q3m_,
            ):
                q3w, q3raw, q3s, q3is = q3w_.ap(), q3raw_.ap(), q3s_.ap(), q3is_.ap()
                q3t, q3x, q3o, q3m = q3t_.ap(), q3x_.ap(), q3o_.ap(), q3m_.ap()
                for it in range(IT3):
                    c.barrier_queue(nc.sync, dma=True)
                    wl = c.dma(nc.sync, q3raw[:, :], w_h[it * 128:(it + 1) * 128, :], "sq")
                    c.wait(nc.vector, "sq", wl)
                    wv3 = q3raw.rearrange("p (a b) -> p a b", b=32)
                    c.vop(nc.vector.tensor_reduce(q3s[:, :], wv3, AX.X, ALU.max, apply_absolute_value=True))
                    c.vop(nc.vector.tensor_scalar(q3s[:, :], q3s[:, :], 1.0 / 7.0, 1e-12, ALU.mult, ALU.max))
                    c.vop(nc.vector.reciprocal(q3is[:, :], q3s[:, :]))
                    isb = q3is.rearrange("p (a b) -> p a b", b=1).broadcast_to((128, 32, 32))
                    sb = q3s.rearrange("p (a b) -> p a b", b=1).broadcast_to((128, 32, 32))
                    tv3 = q3t.rearrange("p (a b) -> p a b", b=32)
                    nc.vector.tensor_tensor(tv3, wv3, isb, ALU.mult)
                    nc.vector.tensor_scalar(q3t[:, :], q3t[:, :], MAGIC, -MAGIC, ALU.add, ALU.add)
                    dq = c.inc(nc.vector.tensor_tensor(tv3, tv3, sb, ALU.mult), "vec")
                    c.wait(nc.tensor, "vec", dq)
                    for ht in range(KT):
                        tp = c.inc(nc.tensor.transpose(pool[2][:, 0:128], q3t[:, ht * 128:(ht + 1) * 128], ident), "pe")
                        c.wait(nc.scalar, "pe", tp)
                        cpd = c.inc(nc.scalar.copy(q3w[:, ht, it * 128:(it + 1) * 128], pool[2][:, 0:128]), "act")
                        c.wait(nc.tensor, "act", cpd)
                firstm = True
                ostore = [0, 0]
                for ch in range(NCH):
                    t0 = ch * CH
                    c.barrier_queue(nc.sync, dma=True)
                    for k in range(KT):
                        c.dma(nc.sync, q3x[:, k, :], S_xq[k * 128:(k + 1) * 128, t0:t0 + CH], "sq")
                    c.wait(nc.tensor, "sq")
                    for m in range(IT3):
                        ps = pool[3 + (m % 2)]
                        c.wait(nc.tensor, "vec")
                        last = None
                        for k in range(KT):
                            last = nc.tensor.matmul(ps[:, :CH], q3w[:, k, m * 128:(m + 1) * 128], q3x[:, k, :],
                                                    start=(k == 0), stop=(k == KT - 1))
                        mm = c.inc(last, "pe")
                        ob = q3o[:, m % 2, :]
                        c.wait(nc.vector, "pe", mm)
                        c.wait(nc.vector, "gq", ostore[m % 2])
                        c.vop(nc.vector.tensor_reduce(q3m, ps[:, :CH], AX.X, ALU.max, apply_absolute_value=True))
                        if firstm:
                            c.vop(nc.vector.tensor_copy(qmx[:, mxcol:mxcol + 1], q3m))
                            firstm = False
                        else:
                            c.vop(nc.vector.tensor_tensor(qmx[:, mxcol:mxcol + 1], qmx[:, mxcol:mxcol + 1], q3m, ALU.max))
                        od = c.inc(nc.vector.tensor_copy(ob, ps[:, :CH]), "vec")
                        c.wait(nc.gpsimd, "vec", od)
                        ostore[m % 2] = c.dma(nc.gpsimd, S_dst[m * 128:(m + 1) * 128, t0:t0 + CH], ob, "gq")
        global_max(1, 2, [3, 4])
        make_scale(3, 3, 0, 32767.0)
        make_scale(4, 4, 5, 32767.0)

        with (
            nc.sbuf_tensor("q5_u", [128, 2, CH], F32) as q5u_,
            nc.sbuf_tensor("q5_g", [128, CH], F32) as q5g_,
            nc.sbuf_tensor("q5_s", [128, CH], F32) as q5s_,
            nc.sbuf_tensor("q5_t", [128, CH], F32) as q5t_,
            nc.sbuf_tensor("q5_m", [128, 1], F32) as q5m_,
        ):
            q5u, q5g, q5s, q5t, q5m = q5u_.ap(), q5g_.ap(), q5s_.ap(), q5t_.ap(), q5m_.ap()
            firstm = True
            ostore = [0, 0]
            for ch in range(NCH):
                t0 = ch * CH
                for m in range(IT3):
                    b = m % 2
                    c.barrier_queue(nc.sync, dma=True)
                    c.dma(nc.sync, q5u[:, b, :], S_up[m * 128:(m + 1) * 128, t0:t0 + CH], "sq")
                    gl = c.dma(nc.sync, q5g[:, :], S_gate[m * 128:(m + 1) * 128, t0:t0 + CH], "sq")
                    c.wait(nc.vector, "sq", gl)
                    c.wait(nc.vector, "gq", ostore[b])
                    uq = c.inc(_emit_qdq(nc, c, q5u[:, b, :], qparb[:, 0:1], qparb[:, 3:4], q5t), "vec")
                    c.wait(nc.gpsimd, "vec", uq)
                    ostore[b] = c.dma(nc.gpsimd, S_upq[m * 128:(m + 1) * 128, t0:t0 + CH], q5u[:, b, :], "gq")
                    gq_ = c.inc(_emit_qdq(nc, c, q5g, qparb[:, 5:6], qparb[:, 4:5], q5t), "vec")
                    c.wait(nc.scalar, "vec", gq_)
                    sg = c.inc(nc.scalar.activation(q5s, q5g, AF.Sigmoid), "act")
                    c.wait(nc.vector, "act", sg)
                    nc.vector.tensor_scalar(q5s, q5s, 65536.0, None, ALU.mult)
                    nc.vector.tensor_scalar(q5s, q5s, MAGIC, -MAGIC, ALU.add, ALU.add)
                    nc.vector.tensor_scalar(q5s, q5s, 0.0, 65535.0, ALU.max, ALU.min)
                    nc.vector.tensor_scalar(q5s, q5s, 1.0 / 65536.0, None, ALU.mult)
                    gs = c.inc(nc.vector.tensor_tensor(q5g, q5g, q5s, ALU.mult), "vec")
                    c.vop(nc.vector.tensor_reduce(q5m, q5g, AX.X, ALU.max, apply_absolute_value=True))
                    if firstm:
                        c.vop(nc.vector.tensor_copy(qmx[:, 3:4], q5m))
                        firstm = False
                    else:
                        c.vop(nc.vector.tensor_tensor(qmx[:, 3:4], qmx[:, 3:4], q5m, ALU.max))
                    gsd = c.inc(nc.vector.tensor_copy(q5g, q5g), "vec")
                    c.wait(nc.gpsimd, "vec", gsd)
                    c.dma(nc.gpsimd, S_gs[m * 128:(m + 1) * 128, t0:t0 + CH], q5g, "gq")
        global_max(3, 1, [6])
        make_scale(6, 6, 7, 32767.0)

        with (
            nc.sbuf_tensor("q6_g", [128, 2, CH], F32) as q6g_,
            nc.sbuf_tensor("q6_u", [128, CH], F32) as q6u_,
            nc.sbuf_tensor("q6_t", [128, CH], F32) as q6t_,
            nc.sbuf_tensor("q6_m", [128, 1], F32) as q6m_,
        ):
            q6g, q6u, q6t, q6m = q6g_.ap(), q6u_.ap(), q6t_.ap(), q6m_.ap()
            firstm = True
            ostore = [0, 0]
            for ch in range(NCH):
                t0 = ch * CH
                for m in range(IT3):
                    b = m % 2
                    c.barrier_queue(nc.sync, dma=True)
                    c.dma(nc.sync, q6g[:, b, :], S_gs[m * 128:(m + 1) * 128, t0:t0 + CH], "sq")
                    ul = c.dma(nc.sync, q6u[:, :], S_upq[m * 128:(m + 1) * 128, t0:t0 + CH], "sq")
                    c.wait(nc.vector, "sq", ul)
                    c.wait(nc.vector, "gq", ostore[b])
                    _emit_qdq(nc, c, q6g[:, b, :], qparb[:, 7:8], qparb[:, 6:7], q6t)
                    nc.vector.tensor_tensor(q6g[:, b, :], q6g[:, b, :], q6u, ALU.mult)
                    c.vop(nc.vector.tensor_reduce(q6m, q6g[:, b, :], AX.X, ALU.max, apply_absolute_value=True))
                    if firstm:
                        c.vop(nc.vector.tensor_copy(qmx[:, 0:1], q6m))
                        firstm = False
                    else:
                        c.vop(nc.vector.tensor_tensor(qmx[:, 0:1], qmx[:, 0:1], q6m, ALU.max))
                    od = c.inc(nc.vector.tensor_copy(q6g[:, b, :], q6g[:, b, :]), "vec")
                    c.wait(nc.gpsimd, "vec", od)
                    ostore[b] = c.dma(nc.gpsimd, S_o[m * 128:(m + 1) * 128, t0:t0 + CH], q6g[:, b, :], "gq")
        global_max(0, 1, [1])
        make_scale(1, 1, 2, 32767.0)

        with (
            nc.sbuf_tensor("q7w", [128, IT3, D], F32) as q7w_,
            nc.sbuf_tensor("q7raw", [128, I3], F32) as q7raw_,
            nc.sbuf_tensor("q7s", [128, 96], F32) as q7s_,
            nc.sbuf_tensor("q7is", [128, 96], F32) as q7is_,
            nc.sbuf_tensor("q7t", [128, I3], F32) as q7t_,
            nc.sbuf_tensor("q7o", [128, IT3, CH], F32) as q7o_,
            nc.sbuf_tensor("q7tt", [128, CH], F32) as q7tt_,
            nc.sbuf_tensor("q7x", [128, 2, CH], F32) as q7x_,
        ):
            q7w, q7raw, q7s, q7is = q7w_.ap(), q7raw_.ap(), q7s_.ap(), q7is_.ap()
            q7t, q7o, q7tt, q7x = q7t_.ap(), q7o_.ap(), q7tt_.ap(), q7x_.ap()
            for ht in range(KT):
                c.barrier_queue(nc.sync, dma=True)
                wl = c.dma(nc.sync, q7raw[:, :], wd_h[ht * 128:(ht + 1) * 128, :], "sq")
                c.wait(nc.vector, "sq", wl)
                wv3 = q7raw.rearrange("p (a b) -> p a b", b=32)
                c.vop(nc.vector.tensor_reduce(q7s[:, :], wv3, AX.X, ALU.max, apply_absolute_value=True))
                c.vop(nc.vector.tensor_scalar(q7s[:, :], q7s[:, :], 1.0 / 7.0, 1e-12, ALU.mult, ALU.max))
                c.vop(nc.vector.reciprocal(q7is[:, :], q7s[:, :]))
                isb = q7is.rearrange("p (a b) -> p a b", b=1).broadcast_to((128, 96, 32))
                sb = q7s.rearrange("p (a b) -> p a b", b=1).broadcast_to((128, 96, 32))
                tv3 = q7t.rearrange("p (a b) -> p a b", b=32)
                nc.vector.tensor_tensor(tv3, wv3, isb, ALU.mult)
                nc.vector.tensor_scalar(q7t[:, :], q7t[:, :], MAGIC, -MAGIC, ALU.add, ALU.add)
                dq = c.inc(nc.vector.tensor_tensor(tv3, tv3, sb, ALU.mult), "vec")
                c.wait(nc.tensor, "vec", dq)
                for it in range(IT3):
                    tp = c.inc(nc.tensor.transpose(pool[2][:, 0:128], q7t[:, it * 128:(it + 1) * 128], ident), "pe")
                    c.wait(nc.scalar, "pe", tp)
                    cpd = c.inc(nc.scalar.copy(q7w[:, it, ht * 128:(ht + 1) * 128], pool[2][:, 0:128]), "act")
                    c.wait(nc.tensor, "act", cpd)
            ostore = [0, 0]
            for ch in range(NCH):
                t0 = ch * CH
                c.barrier_queue(nc.sync, dma=True)
                for k in range(IT3):
                    c.dma(nc.sync, q7o[:, k, :], S_o[k * 128:(k + 1) * 128, t0:t0 + CH], "sq")
                c.wait(nc.vector, "sq")
                last = None
                for k in range(IT3):
                    last = _emit_qdq(nc, c, q7o[:, k, :], qparb[:, 2:3], qparb[:, 1:2], q7tt)
                oq = c.inc(last, "vec")
                c.wait(nc.tensor, "vec", oq)
                for m in range(KT):
                    b = m % 2
                    ps = pool[3 + b]
                    c.wait(nc.tensor, "vec")
                    c.wait(nc.sync, "vec")
                    xl = c.dma(nc.sync, q7x[:, b, :], XH[m * 128:(m + 1) * 128, t0:t0 + CH], "sq")
                    last = None
                    for k in range(IT3):
                        last = nc.tensor.matmul(ps[:, :CH], q7w[:, k, m * 128:(m + 1) * 128], q7o[:, k, :],
                                                start=(k == 0), stop=(k == IT3 - 1))
                    mm = c.inc(last, "pe")
                    c.wait(nc.vector, "pe", mm)
                    c.wait(nc.vector, "sq", xl)
                    c.wait(nc.vector, "gq", ostore[b])
                    od = c.inc(nc.vector.tensor_tensor(q7x[:, b, :], q7x[:, b, :], ps[:, :CH], ALU.add), "vec")
                    c.wait(nc.gpsimd, "vec", od)
                    ostore[b] = c.dma(nc.gpsimd, XH[m * 128:(m + 1) * 128, t0:t0 + CH], q7x[:, b, :], "gq")




def emit_l3_attn(nc, c, T, pool, ones128, ident, onesrow, XH, wT_h, wout_h,
                 tril_h, triu_h, S_xn):
    NH = 16
    NCH = T // 128

    emit_ln_phase(nc, c, T, pool, ones128, XH, S_xn, "l3a")

    import contextlib
    with contextlib.ExitStack() as st:
        def sb(name, shape):
            return st.enter_context(nc.sbuf_tensor(name, shape, F32)).ap()
        w3 = sb("a3_w", [128, KT, 3072])
        wo3 = sb("a3_wo", [128, KT, D])
        tril = sb("a3_tril", [128, 128])
        triu = sb("a3_triu", [128, 128])
        xn3 = sb("a3_xn", [128, KT, 128])
        q3 = sb("a3_q", [128, 1024])
        k3 = sb("a3_k", [128, 1024])
        v3 = sb("a3_v", [128, 1024])
        t16 = sb("a3_t16", [128, 16])
        knr = sb("a3_knr", [128, 1024])
        qsf = sb("a3_qsf", [64, NH, 128])
        kef = sb("a3_kef", [64, NH, 128])
        at3 = sb("a3_at", [128, 128])
        Ssb = sb("a3_S", [64, NH * 64])
        carry = sb("a3_carry", [1, 1024])
        y3 = sb("a3_y", [128, 1024])
        yf = sb("a3_yf", [128, KT, 128])
        x3 = sb("a3_x", [128, KT, 128])
        zr = sb("a3_zr", [1, 512])
        o3 = sb("a3_o", [128, 128])
        c.barrier_queue(nc.sync, dma=True)
        for k in range(KT):
            c.dma(nc.sync, w3[:, k, :], wT_h[k * 128:(k + 1) * 128, :], "sq")
            c.dma(nc.sync, wo3[:, k, :], wout_h[k * 128:(k + 1) * 128, :], "sq")
        c.dma(nc.sync, tril[:, :], tril_h[:, :], "sq")
        c.dma(nc.sync, triu[:, :], triu_h[:, :], "sq")
        nc.vector.memset(carry[:, :], 0.0)
        nc.vector.memset(Ssb[:, :], 0.0)
        c.inc(nc.vector.memset(zr[:, :], 0.0), "vec")
        c.wait(nc.tensor, "vec")

        for ch in range(NCH):
            t0 = ch * 128
            c.barrier_queue(nc.sync, dma=True)
            for k in range(KT):
                c.dma(nc.sync, xn3[:, k, :], S_xn[k * 128:(k + 1) * 128, t0:t0 + 128], "sq")
                c.dma(nc.sync, x3[:, k, :], XH[k * 128:(k + 1) * 128, t0:t0 + 128], "sq")
            c.wait(nc.tensor, "sq")
            for which, dst in ((0, q3), (1, k3), (2, v3)):
                for n in range(2):
                    ps = pool[n % 2]
                    c.wait(nc.tensor, "act")
                    c.wait(nc.tensor, "vec")
                    last = None
                    for k in range(KT):
                        last = nc.tensor.matmul(ps[:, :512], xn3[:, k, :],
                                                w3[:, k, which * 1024 + n * 512:which * 1024 + (n + 1) * 512],
                                                start=(k == 0), stop=(k == KT - 1))
                    mm = c.inc(last, "pe")
                    c.wait(nc.scalar, "pe", mm)
                    if which == 1:
                        c.inc(nc.scalar.activation(dst[:, n * 512:(n + 1) * 512], ps[:, :512], AF.Exp, scale=0.125), "act")
                    else:
                        c.inc(nc.scalar.copy(dst[:, n * 512:(n + 1) * 512], ps[:, :512]), "act")
            c.wait(nc.vector, "act")
            ked = c.inc(nc.vector.tensor_scalar(k3[:, :], k3[:, :], 1e-6, None, ALU.add), "vec")

            c.wait(nc.tensor, "vec", ked)
            c.wait(nc.tensor, "act")
            for n in range(2):
                psn = pool[2 + n]
                nc.tensor.matmul(psn[:, :512], tril[:, :], k3[:, n * 512:(n + 1) * 512], start=True, stop=False)
                nc.tensor.matmul(psn[:, :512], onesrow[0:1, 0:128], carry[0:1, n * 512:(n + 1) * 512],
                                 start=False, stop=True)
            nc.tensor.matmul(pool[5][0:1, 0:512], ones128[:, 0:1], k3[:, 0:512], start=True, stop=True)
            kn = c.inc(nc.tensor.matmul(pool[5][0:1, 512:1024][0:1, 0:0 + 512] if False else pool[4][0:1, 0:512],
                                        ones128[:, 0:1], k3[:, 512:1024], start=True, stop=True), "pe")

            c.wait(nc.vector, "pe", kn)
            nc.vector.tensor_scalar(q3[:, :], q3[:, :], 0.125, None, ALU.mult)
            qv = q3.rearrange("p (h l) -> p h l", l=64)
            c.vop(nc.vector.tensor_reduce(t16[:, :], qv, AX.X, ALU.max))
            t16b = t16.rearrange("p (h o) -> p h o", o=1).broadcast_to((128, 16, 64))
            sm = c.inc(nc.vector.tensor_tensor(qv, qv, t16b, ALU.subtract), "vec")
            c.wait(nc.scalar, "vec", sm)
            ex = c.inc(nc.scalar.activation(q3[:, :], q3[:, :], AF.Exp), "act")
            c.wait(nc.vector, "act", ex)
            c.vop(nc.vector.tensor_reduce(t16[:, :], qv, AX.X, ALU.add))
            c.vop(nc.vector.reciprocal(t16[:, :], t16[:, :]))
            nc.vector.tensor_tensor(qv, qv, t16b, ALU.mult)
            nc.vector.reciprocal(knr[:, 0:512], pool[2][:, :512])
            nc.vector.reciprocal(knr[:, 512:1024], pool[3][:, :512])
            qsd = c.inc(nc.vector.tensor_tensor(q3[:, :], q3[:, :], knr[:, :], ALU.mult), "vec")
            c.vop(nc.vector.tensor_tensor(carry[0:1, 0:512], carry[0:1, 0:512], pool[5][0:1, 0:512], ALU.add))
            cd = c.vop(nc.vector.tensor_tensor(carry[0:1, 512:1024], carry[0:1, 512:1024], pool[4][0:1, 0:512], ALU.add))

            c.wait(nc.tensor, "vec", cd)
            for h in range(NH):
                tp = c.inc(nc.tensor.transpose(pool[4][0:64, 0:128], q3[:, h * 64:(h + 1) * 64], ident), "pe")
                c.wait(nc.scalar, "pe", tp)
                cpd = c.inc(nc.scalar.copy(qsf[:, h, :], pool[4][0:64, 0:128]), "act")
                c.wait(nc.tensor, "act", cpd)
                tp = c.inc(nc.tensor.transpose(pool[4][0:64, 0:128], k3[:, h * 64:(h + 1) * 64], ident), "pe")
                c.wait(nc.scalar, "pe", tp)
                cpd = c.inc(nc.scalar.copy(kef[:, h, :], pool[4][0:64, 0:128]), "act")
                c.wait(nc.tensor, "act", cpd)

            for h in range(NH):
                ybank = pool[0] if h < 8 else pool[1]
                ycol = (h % 8) * 64
                c.wait(nc.tensor, "vec")
                amm = c.inc(nc.tensor.matmul(pool[4][0:128, 0:128], kef[:, h, :], qsf[:, h, :],
                                             start=True, stop=True), "pe")
                c.wait(nc.vector, "pe", amm)
                msk = c.inc(nc.vector.tensor_tensor(at3[:, :], pool[4][0:128, 0:128], triu, ALU.mult), "vec")
                c.wait(nc.tensor, "vec", msk)
                nc.tensor.matmul(ybank[:, ycol:ycol + 64], at3[:, :], v3[:, h * 64:(h + 1) * 64],
                                 start=True, stop=(ch == 0))
                if ch > 0:
                    nc.tensor.matmul(ybank[:, ycol:ycol + 64], qsf[:, h, :], Ssb[:, h * 64:(h + 1) * 64],
                                     start=False, stop=True)
                Sps = pool[6] if h < 8 else pool[7]
                c.inc(nc.tensor.matmul(Sps[0:64, ycol:ycol + 64], k3[:, h * 64:(h + 1) * 64],
                                       v3[:, h * 64:(h + 1) * 64], start=(h % 8 == 0), stop=(h % 8 == 7)), "pe")
            yd = c.val("pe")
            c.wait(nc.vector, "pe", yd)
            nc.vector.tensor_copy(y3[:, 0:512], pool[0][:, :512])
            nc.vector.tensor_copy(y3[:, 512:1024], pool[1][:, :512])
            nc.vector.tensor_tensor(Ssb[:, 0:512], Ssb[:, 0:512], pool[6][0:64, :512], ALU.add)
            yc = c.inc(nc.vector.tensor_tensor(Ssb[:, 512:1024], Ssb[:, 512:1024], pool[7][0:64, :512], ALU.add), "vec")

            c.wait(nc.tensor, "vec", yc)
            for k in range(KT):
                tp = c.inc(nc.tensor.transpose(pool[4][0:128, 0:128], y3[:, k * 128:(k + 1) * 128], ident), "pe")
                c.wait(nc.scalar, "pe", tp)
                cpd = c.inc(nc.scalar.copy(yf[:, k, :], pool[4][0:128, 0:128]), "act")
                c.wait(nc.tensor, "act", cpd)
            for m in range(KT):
                ps = pool[2 + (m % 2)]
                c.wait(nc.tensor, "vec")
                last = None
                for k in range(KT):
                    last = nc.tensor.matmul(ps[:, 0:128], wo3[:, k, m * 128:(m + 1) * 128], yf[:, k, :],
                                            start=(k == 0), stop=(k == KT - 1))
                mm = c.inc(last, "pe")
                c.wait(nc.vector, "pe", mm)
                od = c.inc(nc.vector.tensor_tensor(x3[:, m, :], x3[:, m, :], ps[:, 0:128], ALU.add), "vec")
                c.wait(nc.gpsimd, "vec", od)
                c.dma(nc.gpsimd, XH[m * 128:(m + 1) * 128, t0:t0 + 128], x3[:, m, :], "gq")




def emit_l3f_shift_load(nc, c, dst, XH, t0, CH, first):
    for k in range(KT):
        if k < KT // 2:
            c.dma(nc.sync, dst[:, k, :], XH[k * 128:(k + 1) * 128, t0:t0 + CH], "sq")
        elif not first:
            c.dma(nc.sync, dst[:, k, :], XH[k * 128:(k + 1) * 128, t0 - 1:t0 + CH - 1], "sq")
        else:
            c.dma(nc.sync, dst[:, k, 1:CH], XH[k * 128:(k + 1) * 128, 0:CH - 1], "sq")
    if first:
        last = None
        for k in range(KT // 2, KT):
            last = nc.vector.memset(dst[:, k, 0:1], 0.0)
        c.inc(last, "vec")


def emit_l3_ffn_layer(nc, c, T, pool, ones128, ident, onesrow, streams, lyr,
                      wqkv_h, wo_h, rotq_h, rotk_h, lgb_h, w1_h, b1_h, w2_h, b2_h,
                      dmc_h, dmf_h, S_xn, S_a3, S_p3, parts="af"):
    import contextlib
    XA, XB = streams
    NW = T // 512
    H25 = 2730
    MT25 = 22

    if "a" not in parts:
        pass
    else:
     with contextlib.ExitStack() as st:
        def sb(name, shape):
            return st.enter_context(nc.sbuf_tensor("%s_%d" % (name, lyr), shape, F32)).ap()
        wq3 = sb("f3_wqkv", [128, KT, 1536])
        wo3 = sb("f3_wo", [128, 4, D])
        rq3 = sb("f3_rotq", [128, 2, T // 128, 64])
        rk3 = sb("f3_rotk", [128, 2, T // 128, 64])
        dmc = sb("f3_dmc", [128, 128])
        dmf = sb("f3_dmf", [128, 128])
        h3 = sb("f3_h", [128, KT, 512])
        x3 = sb("f3_x", [128, KT, 512])
        qt = sb("f3_q", [128, 512])
        t8 = sb("f3_t8", [128, 8])
        tq2 = sb("f3_tq2", [128, 512])
        vring = sb("f3_v", [128, 8, 512])
        kring = sb("f3_kf", [64, 8, 8, 128])
        qf = sb("f3_qf", [64, 8, 128])
        Ssb3 = sb("f3_S", [128, 640])
        PT3 = sb("f3_PT", [128, 128])
        ao3 = sb("f3_ao", [128, 512])
        of3 = sb("f3_of", [128, 4, 128])
        ox3 = sb("f3_ox", [128, 128])

        c.barrier_queue(nc.sync, dma=True)
        for k in range(KT):
            c.dma(nc.sync, wq3[:, k, :], wqkv_h[lyr, k * 128:(k + 1) * 128, :], "sq")
        for k in range(4):
            c.dma(nc.sync, wo3[:, k, :], wo_h[lyr, k * 128:(k + 1) * 128, :], "sq")
        c.dma(nc.sync, rq3[:, :, :, :], rotq_h[lyr], "sq")
        c.dma(nc.sync, rk3[:, :, :, :], rotk_h[lyr], "sq")
        c.dma(nc.sync, dmc[:, :], dmc_h[:, :], "sq")
        c.dma(nc.sync, dmf[:, :], dmf_h[:, :], "sq")

        for w in range(NW):
            t0 = w * 512
            c.barrier_queue(nc.sync, dma=True)
            emit_l3f_shift_load(nc, c, h3, XA, t0, 512, first=(w == 0))
            for k in range(KT):
                c.dma(nc.sync, x3[:, k, :], XA[k * 128:(k + 1) * 128, t0:t0 + 512], "sq")
            c.wait(nc.tensor, "sq")
            for tt in range(4):
                g = w * 4 + tt
                slot = g % 8
                for part in (1, 2, 0):
                    ps = pool[2]
                    c.wait(nc.tensor, "act")
                    c.wait(nc.tensor, "vec")
                    last = None
                    for k in range(KT):
                        last = nc.tensor.matmul(ps[:, :512], h3[:, k, tt * 128:(tt + 1) * 128],
                                                wq3[:, k, part * 512:(part + 1) * 512],
                                                start=(k == 0), stop=(k == KT - 1))
                    mm = c.inc(last, "pe")
                    c.wait(nc.scalar, "pe", mm)
                    if part == 2:
                        c.wait(nc.scalar, "vec")
                        c.inc(nc.scalar.copy(vring[:, slot, :], ps[:, :512]), "act")
                        continue
                    sq = c.inc(nc.scalar.activation(tq2, ps[:, :512], AF.Square), "act")
                    c.wait(nc.vector, "act", sq)
                    rr = c.vop(nc.vector.tensor_reduce(t8, tq2.rearrange("p (h l) -> p h l", l=64),
                                                       AX.X, ALU.add))
                    c.wait(nc.scalar, "vec", rr)
                    sq2 = c.inc(nc.scalar.activation(t8, t8, AF.Sqrt), "act")
                    c.wait(nc.vector, "act", sq2)
                    c.vop(nc.vector.tensor_scalar(t8, t8, 1e-12, None, ALU.max))
                    c.vop(nc.vector.reciprocal(t8, t8))
                    t8b = t8.rearrange("p (h o) -> p h o", o=1).broadcast_to((128, 8, 64))
                    qv = qt.rearrange("p (h l) -> p h l", l=64)
                    nc.vector.tensor_tensor(qv, ps[:, :512].rearrange("p (h l) -> p h l", l=64), t8b, ALU.mult)
                    rtab = rq3 if part == 0 else rk3
                    cs = rtab[:, 0, g, :].rearrange("p (o l) -> p o l", o=1).broadcast_to((128, 8, 64))
                    ss = rtab[:, 1, g, :].rearrange("p (o l) -> p o l", o=1).broadcast_to((128, 8, 64))
                    sw = qt.rearrange("p (h s l) -> p h s l", s=2, l=32)
                    swr = bass.AP(tensor=sw.tensor, offset=sw.offset + 32,
                                  ap=[list(p) for p in sw.ap[:2]] + [[-32, 2], [1, 32]])
                    nc.vector.tensor_tensor(tq2.rearrange("p (h s l) -> p h s l", s=2, l=32), swr, 
                                            ss.rearrange("p h (s l) -> p h s l", s=2) if False else ss, ALU.mult)
                    nc.vector.tensor_tensor(qv, qv, cs, ALU.mult)
                    rd = c.inc(nc.vector.tensor_tensor(qt, qt, tq2, ALU.add), "vec")
                    c.wait(nc.tensor, "vec", rd)
                    for h in range(8):
                        c.wait(nc.tensor, "act")
                        tp = c.inc(nc.tensor.transpose(pool[4][0:64, 0:128], qt[:, h * 64:(h + 1) * 64], ident), "pe")
                        c.wait(nc.scalar, "pe", tp)
                        if part == 0:
                            cpd = c.inc(nc.scalar.copy(qf[:, h, :], pool[4][0:64, 0:128]), "act")
                        else:
                            cpd = c.inc(nc.scalar.copy(kring[:, h, slot, :], pool[4][0:64, 0:128]), "act")
                    c.wait(nc.tensor, "act", cpd)
                kts = [i for i in range(g - 4, g + 1) if i >= 0]
                nk = len(kts)
                for h in range(8):
                    c.wait(nc.tensor, "act")
                    c.wait(nc.tensor, "vec")
                    for i, kg in enumerate(kts[:-1]):
                        nc.tensor.matmul(pool[5][:, i * 128:(i + 1) * 128], qf[:, h, :],
                                         kring[:, h, kg % 8, :], start=(i == 0), stop=(i == nk - 2))
                    smm = c.inc(nc.tensor.matmul(pool[3][:, 0:128], qf[:, h, :],
                                                 kring[:, h, kts[-1] % 8, :], start=True, stop=True), "pe")
                    c.wait(nc.vector, "pe", smm)
                    if nk > 1:
                        nc.vector.tensor_copy(Ssb3[:, 0:(nk - 1) * 128], pool[5][:, 0:(nk - 1) * 128])
                    nc.vector.tensor_copy(Ssb3[:, (nk - 1) * 128:nk * 128], pool[3][:, 0:128])
                    if nk == 5:
                        nc.vector.tensor_tensor(Ssb3[:, 0:128], Ssb3[:, 0:128], dmf, ALU.add)
                    nc.vector.tensor_tensor(Ssb3[:, (nk - 1) * 128:nk * 128],
                                            Ssb3[:, (nk - 1) * 128:nk * 128], dmc, ALU.add)
                    c.vop(nc.vector.tensor_reduce(t8[:, 0:1], Ssb3[:, 0:nk * 128], AX.X, ALU.max))
                    ng = c.vop(nc.vector.tensor_scalar(t8[:, 1:2], t8[:, 0:1], -1.0, None, ALU.mult))
                    c.wait(nc.scalar, "vec", ng)
                    pe_ = c.inc(nc.scalar.activation(Ssb3[:, 0:nk * 128], Ssb3[:, 0:nk * 128], AF.Exp,
                                                     bias=t8[:, 1:2], accum_out=t8[:, 2:3]), "act")
                    c.wait(nc.vector, "act", pe_)
                    rl = c.inc(nc.vector.reciprocal(t8[:, 3:4], t8[:, 2:3]), "vec")
                    c.wait(nc.vector, "vec", rl)
                    for i, kg in enumerate(kts):
                        c.wait(nc.tensor, "vec" if i == 0 else "act")
                        tp = c.inc(nc.tensor.transpose(pool[4][0:128, 0:128], Ssb3[:, i * 128:(i + 1) * 128], ident), "pe")
                        c.wait(nc.scalar, "pe", tp)
                        cpd = c.inc(nc.scalar.copy(PT3[:, :], pool[4][0:128, 0:128]), "act")
                        c.wait(nc.tensor, "act", cpd)
                        pvmm = nc.tensor.matmul(pool[3][:, 0:64], PT3[:, :], vring[:, kg % 8, h * 64:(h + 1) * 64],
                                                start=(i == 0), stop=(i == nk - 1))
                    pv = c.inc(pvmm, "pe")
                    c.wait(nc.vector, "pe", pv)
                    c.wait(nc.vector, "vec", rl)
                    c.inc(nc.vector.tensor_scalar(tq2[:, h * 64:(h + 1) * 64], pool[3][:, 0:64],
                                                  t8[:, 3:4], None, ALU.mult), "vec")
                od2 = c.inc(nc.vector.tensor_copy(ao3[:, :], tq2[:, :]), "vec")
                c.wait(nc.tensor, "vec", od2)
                for k in range(4):
                    c.wait(nc.tensor, "act")
                    tp = c.inc(nc.tensor.transpose(pool[4][0:128, 0:128], ao3[:, k * 128:(k + 1) * 128], ident), "pe")
                    c.wait(nc.scalar, "pe", tp)
                    cpd = c.inc(nc.scalar.copy(of3[:, k, :], pool[4][0:128, 0:128]), "act")
                c.wait(nc.tensor, "act", cpd)
                for m in range(KT):
                    ps = pool[2 + (m % 2)]
                    c.wait(nc.tensor, "vec")
                    last = None
                    for k in range(4):
                        last = nc.tensor.matmul(ps[:, 0:128], wo3[:, k, m * 128:(m + 1) * 128], of3[:, k, :],
                                                start=(k == 0), stop=(k == 3))
                    mm = c.inc(last, "pe")
                    c.wait(nc.vector, "pe", mm)
                    c.wait(nc.vector, "gq")
                    od = c.inc(nc.vector.tensor_tensor(ox3[:, :], x3[:, m, tt * 128:(tt + 1) * 128],
                                                       ps[:, 0:128], ALU.add), "vec")
                    c.wait(nc.gpsimd, "vec", od)
                    c.dma(nc.gpsimd, XB[m * 128:(m + 1) * 128, t0 + tt * 128:t0 + (tt + 1) * 128], ox3[:, :], "gq")

    CH = 512
    if "f" not in parts:
        return
    with contextlib.ExitStack() as st:
        def sb(name, shape):
            return st.enter_context(nc.sbuf_tensor("%s_%d" % (name, lyr), shape, F32)).ap()
        fh = sb("f4_h", [128, KT, CH])
        fsq = sb("f4_sq", [128, KT, CH])
        fmu = sb("f4_mu", [128, CH])
        fr = sb("f4_r", [128, CH])
        fgb = sb("f4_gb", [128, KT, 2])
        c.barrier_queue(nc.sync, dma=True)
        c.dma(nc.sync, fgb[:, :, :], lgb_h[lyr], "sq")
        for ch in range(T // CH):
            t0 = ch * CH
            c.barrier_queue(nc.sync, dma=True)
            emit_l3f_shift_load(nc, c, fh, XB, t0, CH, first=(ch == 0))
            c.wait(nc.vector, "sq")
            c.wait(nc.scalar, "sq")
            c.wait(nc.tensor, "sq")
            emit_ln_stats(c, ones128, [fh[:, k, :] for k in range(KT)],
                          [fsq[:, k, :] for k in range(KT)], pool[0], pool[1], fmu, fr, CH)
            for k in range(KT):
                nc.vector.tensor_tensor(fh[:, k, :], fh[:, k, :], fmu, ALU.subtract)
                nc.vector.tensor_tensor(fh[:, k, :], fh[:, k, :], fr, ALU.mult)
                hd = c.inc(nc.vector.tensor_scalar(fh[:, k, :], fh[:, k, :], fgb[:, k, 0:1], fgb[:, k, 1:2],
                                                   ALU.mult, ALU.add), "vec")
                c.wait(nc.gpsimd, "vec", hd)
                c.dma(nc.gpsimd, S_xn[k * 128:(k + 1) * 128, t0:t0 + CH], fh[:, k, :], "gq")

    for half in range(2):
        with contextlib.ExitStack() as st:
            def sb(name, shape):
                return st.enter_context(nc.sbuf_tensor("%s_%d_%d" % (name, lyr, half), shape, F32)).ap()
            w1s = sb("f5_w", [128, KT, 2816])
            b1s = sb("f5_b", [128, MT25])
            b1ss = sb("f5_bs", [128, MT25])
            xns = sb("f5_xn", [128, KT, CH])
            av = sb("f5_a", [128, 2, CH])
            ev = sb("f5_e", [128, CH])
            gv = sb("f5_g", [128, CH])
            c.barrier_queue(nc.sync, dma=True)
            for k in range(KT):
                c.dma(nc.sync, w1s[:, k, 0:2816], w1_h[lyr, half, k * 128:(k + 1) * 128, :], "sq")
            bl = c.dma(nc.sync, b1s[:, :], b1_h[lyr, half], "sq")
            ostore = [0, 0]
            for ch in range(T // CH):
                t0 = ch * CH
                c.barrier_queue(nc.sync, dma=True)
                for k in range(KT):
                    c.dma(nc.sync, xns[:, k, :], S_xn[k * 128:(k + 1) * 128, t0:t0 + CH], "sq")
                c.wait(nc.tensor, "sq")
                for m in range(MT25):
                    rows = 42 if m == MT25 - 1 else 128
                    ps = pool[2 + (m % 2)]
                    b = m % 2
                    c.wait(nc.tensor, "act")
                    c.wait(nc.tensor, "vec")
                    if half == 1:
                        c.barrier_queue(nc.sync, dma=False)
                        al = c.dma(nc.sync, av[:, b, :][0:rows, :], S_a3[m * 128:m * 128 + rows, t0:t0 + CH], "sq")
                    last = None
                    for k in range(KT):
                        last = nc.tensor.matmul(ps[0:rows, :CH], w1s[:, k, m * 128:m * 128 + rows], xns[:, k, :],
                                                start=(k == 0), stop=(k == KT - 1))
                    mm = c.inc(last, "pe")
                    c.wait(nc.scalar, "pe", mm)
                    if half == 0:
                        c.wait(nc.scalar, "gq", ostore[b])
                        ev_ = c.inc(nc.scalar.activation(av[:, b, :][0:rows, :], ps[0:rows, :CH], AF.Identity,
                                                         bias=b1s[0:rows, m:m + 1]), "act")
                        c.wait(nc.gpsimd, "act", ev_)
                        ostore[b] = c.dma(nc.gpsimd, S_a3[m * 128:m * 128 + rows, t0:t0 + CH], av[:, b, :][0:rows, :], "gq")
                    else:
                        er = c.inc(nc.scalar.activation(ev[0:rows, :], ps[0:rows, :CH], AF.Gelu,
                                                        bias=b1s[0:rows, m:m + 1]), "act")
                        c.wait(nc.vector, "act", er)
                        c.wait(nc.vector, "sq", al)
                        c.wait(nc.vector, "gq", ostore[b])
                        pd = c.inc(nc.vector.tensor_tensor(av[:, b, :][0:rows, :], av[:, b, :][0:rows, :],
                                                           ev[0:rows, :], ALU.mult), "vec")
                        c.wait(nc.gpsimd, "vec", pd)
                        ostore[b] = c.dma(nc.gpsimd, S_p3[m * 128:m * 128 + rows, t0:t0 + CH], av[:, b, :][0:rows, :], "gq")

    with contextlib.ExitStack() as st:
        def sb(name, shape):
            return st.enter_context(nc.sbuf_tensor("%s_%d" % (name, lyr), shape, F32)).ap()
        w2s = sb("f6_w", [128, MT25, D])
        b2s = sb("f6_b", [128, KT])
        pv2 = sb("f6_p", [128, MT25, CH])
        xv2 = sb("f6_x", [128, 2, CH])
        c.barrier_queue(nc.sync, dma=True)
        for k in range(MT25):
            rows = 42 if k == MT25 - 1 else 128
            c.dma(nc.sync, w2s[0:rows, k, :], w2_h[lyr, k * 128:k * 128 + rows, :], "sq")
        c.dma(nc.sync, b2s[:, :], b2_h[lyr], "sq")
        ostore = [0, 0]
        for ch in range(T // CH):
            t0 = ch * CH
            c.barrier_queue(nc.sync, dma=True)
            for k in range(MT25):
                rows = 42 if k == MT25 - 1 else 128
                c.dma(nc.sync, pv2[0:rows, k, :], S_p3[k * 128:k * 128 + rows, t0:t0 + CH], "sq")
            c.wait(nc.tensor, "sq")
            for m in range(KT):
                b = m % 2
                ps = pool[2 + b]
                c.wait(nc.tensor, "vec")
                c.wait(nc.sync, "vec")
                xl = c.dma(nc.sync, xv2[:, b, :], XB[m * 128:(m + 1) * 128, t0:t0 + CH], "sq")
                last = None
                for k in range(MT25):
                    rows = 42 if k == MT25 - 1 else 128
                    last = nc.tensor.matmul(ps[:, :CH], w2s[0:rows, k, m * 128:(m + 1) * 128], pv2[0:rows, k, :],
                                            start=(k == 0), stop=(k == MT25 - 1))
                mm = c.inc(last, "pe")
                c.wait(nc.vector, "pe", mm)
                c.wait(nc.vector, "sq", xl)
                c.wait(nc.vector, "gq", ostore[b])
                nc.vector.scalar_tensor_tensor(xv2[:, b, :], ps[:, :CH], b2s[:, m:m + 1], xv2[:, b, :],
                                               ALU.add, ALU.add)
                od = c.inc(nc.vector.tensor_copy(xv2[:, b, :], xv2[:, b, :]), "vec")
                c.wait(nc.gpsimd, "vec", od)
                ostore[b] = c.dma(nc.gpsimd, XA[m * 128:(m + 1) * 128, t0:t0 + CH], xv2[:, b, :], "gq")




def _fm(v):
    v = np.asarray(v, np.float32)
    n = v.shape[0] // 128
    return np.ascontiguousarray(v.reshape(n, 128).T)


def _prep_inputs(inp):
    g = {}
    g["identH"] = np.eye(128, dtype=np.float32)
    g["l0_w_in"] = np.ascontiguousarray(inp["L0_attn_W_in"], np.float32)
    g["l0_b_in"] = _fm(inp["L0_attn_b_in"])
    cw = np.asarray(inp["L0_attn_conv_w"], np.float32)[:, 0, :]
    g["l0_convw"] = np.ascontiguousarray(cw.reshape(16, 128, 3).transpose(1, 0, 2))
    g["l0_convb"] = _fm(inp["L0_attn_conv_b"])
    g["l0_wa"] = np.ascontiguousarray(inp["L0_attn_Wa"], np.float32)
    g["l0_wb"] = np.ascontiguousarray(inp["L0_attn_Wb"], np.float32)
    g["l0_wc"] = np.ascontiguousarray(inp["L0_attn_Wc"], np.float32)
    g["l0_babc"] = np.ascontiguousarray(
        np.stack([_fm(inp["L0_attn_ba"]), _fm(inp["L0_attn_bb"]), _fm(inp["L0_attn_bc"])], axis=0))
    g["l0_wo"] = np.ascontiguousarray(inp["L0_attn_Wo"], np.float32)
    g["l0_bo"] = _fm(inp["L0_attn_bo"])
    g["l0_gb"] = np.ascontiguousarray(
        np.stack([_fm(inp["L0_attn_gamma"]), _fm(inp["L0_attn_beta"])], axis=2))
    g["l0f_sgT"] = np.ascontiguousarray(np.asarray(inp["L0_ffn_shared_gate"], np.float32).T)
    g["l0f_suT"] = np.ascontiguousarray(np.asarray(inp["L0_ffn_shared_up"], np.float32).T)
    g["l0f_sdT"] = np.ascontiguousarray(np.asarray(inp["L0_ffn_shared_down"], np.float32).T)
    g["l0f_gate"] = np.ascontiguousarray(np.asarray(inp["L0_ffn_gate_w"], np.float32))
    g["l0f_up"] = np.ascontiguousarray(np.asarray(inp["L0_ffn_up_w"], np.float32))
    g["l0f_down"] = np.ascontiguousarray(np.asarray(inp["L0_ffn_down_w"], np.float32))
    g["l0f_rwT"] = np.ascontiguousarray(np.asarray(inp["L0_ffn_router_w"], np.float32).T)
    g["l0f_rb"] = np.ascontiguousarray(np.asarray(inp["L0_ffn_routing_bias"], np.float32)[None, :])
    g["l1_wq"] = np.ascontiguousarray(inp["L1_attn_Wq"], np.float32)
    g["l1_wk"] = np.ascontiguousarray(inp["L1_attn_Wk"], np.float32)
    g["l1_wv"] = np.ascontiguousarray(inp["L1_attn_Wv"], np.float32)
    g["l1_wo"] = np.ascontiguousarray(inp["L1_attn_Wo"], np.float32)
    g["l1_bq"] = _fm(inp["L1_attn_bq"])
    g["l1_bk"] = _fm(inp["L1_attn_bk"])
    g["l1_bv"] = np.ascontiguousarray(np.asarray(inp["L1_attn_bv"], np.float32)[None, :])
    g["l1_bo"] = _fm(inp["L1_attn_bo"])
    rw = np.asarray(inp["L1_attn_rw"], np.float32)
    rw2 = np.zeros((128, 16, 64), np.float32)
    rw2[0:64] = rw.transpose(1, 0, 2)
    rw2[64:128] = rw.transpose(1, 0, 2)
    g["l1_rw2"] = rw2
    g["l1f_w"] = np.ascontiguousarray(inp["L1_ffn_weight"], np.float32)
    g["l1f_a1"] = np.ascontiguousarray(inp["L1_ffn_aux_weight1"], np.float32)
    g["l1f_a2"] = np.ascontiguousarray(inp["L1_ffn_aux_weight2"], np.float32)
    g["l1f_b"] = _fm(inp["L1_ffn_bias"])
    g["l2_w"] = np.ascontiguousarray(inp["L2_attn_W"], np.float32)
    dm = np.zeros((128, 128), np.float32)
    ii, jj = np.mgrid[0:128, 0:128]
    dm[jj < ii] = -9.0e15
    g["l2_dmask"] = dm
    g["l2f_wg"] = np.ascontiguousarray(inp["L2_ffn_w_gate"], np.float32)
    g["l2f_wu"] = np.ascontiguousarray(inp["L2_ffn_w_up"], np.float32)
    g["l2f_wd"] = np.ascontiguousarray(inp["L2_ffn_w_down"], np.float32)
    g["l3_wT"] = np.ascontiguousarray(np.asarray(inp["L3_attn_w"], np.float32).T)
    g["l3_wout"] = np.ascontiguousarray(inp["L3_attn_w_out"], np.float32)
    ii, jj = np.mgrid[0:128, 0:128]
    g["l3_tril"] = (ii <= jj).astype(np.float32)
    g["l3_triu"] = (ii <= jj).astype(np.float32)

    g["l3f_wqkv"] = np.ascontiguousarray(inp["L3_ffn_wqkv"], np.float32)
    g["l3f_wo"] = np.ascontiguousarray(inp["L3_ffn_wo"], np.float32)
    Tn = int(np.asarray(inp["L0_attn_x"]).shape[1])
    DIM_HEAD, WINDOW = 64, 512
    half_idx = np.arange(0, DIM_HEAD, 2, dtype=np.float64)
    inv_freq = 1.0 / 10000.0 ** (half_idx / DIM_HEAD)
    tpos = np.arange(Tn, dtype=np.float64)
    freqs = tpos[:, None] * inv_freq[None, :]
    freqs = np.concatenate([freqs, freqs], axis=-1)
    base = (half_idx + 0.4 * DIM_HEAD) / (1.4 * DIM_HEAD)
    power = (tpos - Tn // 2) / float(WINDOW // 2)
    scl = base[None, :] ** power[:, None]
    scl = np.concatenate([scl, scl], axis=-1)
    cosf, sinf = np.cos(freqs), np.sin(freqs)

    def rot_tables(vec_scale, sc, qk8):
        vs = np.asarray(vec_scale, np.float64)
        cs = cosf * sc * vs[None, :] * qk8
        sgn = np.concatenate([-vs[32:], vs[:32]])
        ss = sinf * sc * sgn[None, :] * qk8
        cs4 = cs.reshape(Tn // 128, 128, 64).transpose(1, 0, 2)
        ss4 = ss.reshape(Tn // 128, 128, 64).transpose(1, 0, 2)
        return np.stack([cs4, ss4], axis=1).astype(np.float32)

    qs2 = np.asarray(inp["L3_ffn_q_scale"], np.float64)
    ks2 = np.asarray(inp["L3_ffn_k_scale"], np.float64)
    g["l3f_rotq"] = np.ascontiguousarray(np.stack([rot_tables(qs2[l], scl, 8.0) for l in range(2)]))
    g["l3f_rotk"] = np.ascontiguousarray(np.stack([rot_tables(ks2[l], 1.0 / scl, 1.0) for l in range(2)]))
    g["l3f_lgb"] = np.ascontiguousarray(np.stack([
        np.stack([_fm(inp["L3_ffn_ln_g"][l]), _fm(inp["L3_ffn_ln_b"][l])], axis=2) for l in range(2)]))
    w1 = np.asarray(inp["L3_ffn_w1"], np.float32)
    w1p = np.zeros((2, 2, 1024, 2816), np.float32)
    w1p[:, 0, :, 0:2730] = w1[:, :, 0:2730]
    w1p[:, 1, :, 0:2730] = w1[:, :, 2730:5460]
    g["l3f_w1"] = np.ascontiguousarray(w1p)
    b1 = np.asarray(inp["L3_ffn_b1"], np.float32)
    b1p = np.zeros((2, 2, 128, 22), np.float32)
    for l in range(2):
        for hf in range(2):
            v = np.zeros(2816, np.float32)
            v[0:2730] = b1[l, hf * 2730:(hf + 1) * 2730]
            b1p[l, hf] = v.reshape(22, 128).T
    g["l3f_b1"] = np.ascontiguousarray(b1p)
    g["l3f_w2"] = np.ascontiguousarray(inp["L3_ffn_w2"], np.float32)
    g["l3f_b2"] = np.ascontiguousarray(np.stack([_fm(inp["L3_ffn_b2"][l]) for l in range(2)]))
    NEG = np.float32(-3.4028235e38)
    dmc2 = np.zeros((128, 128), np.float32)
    dmc2[jj > ii] = NEG
    g["l3f_dmc"] = dmc2
    dmf2 = np.zeros((128, 128), np.float32)
    dmf2[jj < ii] = NEG
    g["l3f_dmf"] = dmf2

    x = np.asarray(inp["L0_attn_x"], np.float32)
    in_maps = []
    for b in range(x.shape[0]):
        m = dict(g)
        m["xin"] = np.ascontiguousarray(x[b].T)
        in_maps.append(m)
    return in_maps


def kernel(**inp):
    in_maps = _prep_inputs(inp)
    nc = build_program(T=4096)
    res = run_bass_kernel_spmd(nc, in_maps, core_ids=list(range(8)))
    out = np.stack([np.ascontiguousarray(r["yout"].T) for r in res.results])
    return out
